# version 01
# baseline (speedup 1.0000x reference)
import numpy as np
import concourse.bass as bass
import concourse.mybir as mybir
from concourse.bass_utils import run_bass_kernel_spmd

F32 = mybir.dt.float32
BF16 = mybir.dt.bfloat16
U8 = mybir.dt.uint8
AF = mybir.ActivationFunctionType
OP = mybir.AluOpType
AX = mybir.AxisListType.X

NCORES = 8
T = 16384
TC = T // NCORES
D = 1024
KT = D // 128

ENGINES = ("pe", "act", "dve", "pool", "sp")

class Tok:
    __slots__ = ("eng", "idx", "is_dma", "seq", "dma_seq")
    def __init__(self, eng, idx, is_dma):
        self.eng = eng; self.idx = idx; self.is_dma = is_dma
        self.seq = None; self.dma_seq = None

class FTok:
    def __init__(self, tok):
        self.tok = tok

class Prog:
    def __init__(self, nc):
        self.nc = nc
        self.ops = []
        self.last = {e: None for e in ENGINES}
        self.frontier = []

    def _add(self, eng, emit, deps, is_dma, nobar):
        tok = Tok(eng, len(self.ops), is_dma)
        d = [x for x in deps if x is not None]
        if not nobar:
            d += self.frontier
        self.ops.append((eng, emit, d, is_dma, tok))
        self.last[eng] = tok
        return tok

    def op(self, eng, meth, *args, deps=(), nobar=False, **kw):
        return self._add(eng, lambda e: getattr(e, meth)(*args, **kw), deps, False, nobar)

    def add(self, eng, emit, deps=(), nobar=False):
        return self._add(eng, emit, deps, False, nobar)

    def dma(self, eng, out, in_, deps=(), nobar=False, **kw):
        return self._add(eng, lambda e: e.dma_start(out=out, in_=in_, **kw),
                         deps, True, nobar)

    def step(self):
        self.frontier = [t for t in self.last.values() if t is not None]

    def finalize(self):
        nc = self.nc
        eng_count = {e: 0 for e in ENGINES}
        dma_count = {e: 0 for e in ENGINES}
        for (eng, emit, deps, is_dma, tok) in self.ops:
            if is_dma:
                tok.dma_seq = dma_count[eng]; dma_count[eng] += 1
            tok.seq = eng_count[eng]; eng_count[eng] += 1
        need_inc = set()
        for (eng, emit, deps, is_dma, tok) in self.ops:
            for d in deps:
                if isinstance(d, FTok):
                    need_inc.add(d.tok.idx)
                elif d.is_dma or d.eng != eng or eng != "pe":
                    need_inc.add(d.idx)
        sems = {}; dsems = {}
        for e in ENGINES:
            if eng_count[e]:
                sems[e] = nc.semaphore(f"s_{e}").__enter__()
            if dma_count[e]:
                dsems[e] = nc.semaphore(f"d_{e}").__enter__()
        inc_upto = {}
        run = {e: 0 for e in ENGINES}
        for (eng, emit, deps, is_dma, tok) in self.ops:
            if (not is_dma) and tok.idx in need_inc:
                run[eng] += 1
            inc_upto[tok.idx] = run[eng]
        final_inc = dict(run)
        per_eng = {e: [] for e in ENGINES}
        for rec in self.ops:
            per_eng[rec[0]].append(rec)

        blk = nc.Block().__enter__()

        def make_runner(e, is_final_waiter):
            def runner(engine):
                known = {f: 0 for f in ENGINES}
                known_d = {f: 0 for f in ENGINES}
                for (eng, emit, deps, is_dma, tok) in per_eng[e]:
                    for d in deps:
                        forced = isinstance(d, FTok)
                        if forced:
                            d = d.tok
                        if d.eng == e and not d.is_dma and not forced and e == "pe":
                            continue
                        if d.is_dma:
                            v = 16 * (d.dma_seq + 1)
                            if known_d[d.eng] < v:
                                engine.wait_ge(dsems[d.eng], v)
                                known_d[d.eng] = v
                        else:
                            v = inc_upto[d.idx]
                            assert v > 0, "dep on op without inc"
                            if known[d.eng] < v:
                                engine.wait_ge(sems[d.eng], v)
                                known[d.eng] = v
                    inst = emit(engine)
                    if is_dma:
                        inst.then_inc(dsems[e], 16)
                    elif tok.idx in need_inc:
                        inst.then_inc(sems[e], 1)
                if is_final_waiter:
                    for f in ENGINES:
                        if dma_count[f]:
                            engine.wait_ge(dsems[f], 16 * dma_count[f])
                    for f in ENGINES:
                        if f != e and final_inc[f]:
                            engine.wait_ge(sems[f], final_inc[f])
            return runner

        names = {"pe": "tensor", "act": "scalar", "dve": "vector",
                 "pool": "gpsimd", "sp": "sync"}
        for e in ENGINES:
            if per_eng[e] or e == "pool":
                getattr(blk, names[e])(make_runner(e, e == "pool"))
        blk.__exit__(None, None, None)

def host_constants():
    cst = np.zeros((128, 8), np.float32)
    cst[:, 0] = 1.0
    cst[:, 1] = 1.0 / 1024.0
    cst[0, 2] = 1e-5
    cst[0, 3] = 1e-6
    cst[:, 4] = 0.5
    cst[:, 5] = -1e30
    cst[:, 6] = float(np.log(1e-6))
    onesrow = np.ones((1, 128), np.float32)
    eye = np.eye(128, dtype=np.float32)
    trilI = np.tril(np.ones((128, 128), np.float32))
    triuI = np.triu(np.ones((128, 128), np.float32))
    return {"cst": cst, "eye": eye, "trilI": trilI, "triuI": triuI,
            "onesrow": onesrow}

class Ctx:
    def __init__(self):
        self.nc = bass.Bass("TRN2", target_bir_lowering=False)
        self.pr = Prog(self.nc)
        self._uid = 0
        self.cst_d = self.inp("cst", [128, 8])
        self.eye_d = self.inp("eye", [128, 128])
        self.trilI_d = self.inp("trilI", [128, 128])
        self.onesrow_d = self.inp("onesrow", [1, 128])
        self.cst = self.sb("c_cst", 128, 8)
        self.eye = self.sb("c_eye", 128, 128)
        self.eyeb = self.sb("c_eyeb", 128, 128, BF16)
        self.trilIb = self.sb("c_trilb", 128, 128, BF16)
        self.onesrow = self.sb("c_or", 1, 128)
        self.cstb = self.sb("c_cstb", 128, 8, BF16)
        self.pr.dma("sp", self.cst[:, :], self.cst_d[:, :])
        self.pr.dma("sp", self.eye[:, :], self.eye_d[:, :])
        self.pr.dma("pool", self.eyeb[:, :], self.eye_d[:, :])
        self.pr.dma("pool", self.trilIb[:, :], self.trilI_d[:, :])
        self.pr.dma("sp", self.onesrow[:, :], self.onesrow_d[:, :])
        self.pr.dma("pool", self.cstb[:, :], self.cst_d[:, :])
        self.zeros8 = self.sb("c_z8", 128, 8)
        self.pr.op("dve", "memset", self.zeros8[:, :], 0.0)
        self.pr.step()
        self.one_col = self.cst[:, 0:1]
        self.inv_col = self.cst[:, 1:2]
        self.half_col = self.cst[:, 4:5]
        self.neg_col = self.cst[:, 5:6]
        self.ln6_col = self.cst[:, 6:7]
        self.invb_col = self.cstb[:, 1:2]
        self.oneb_col = self.cstb[:, 0:1]
        self.eps11 = self.cst[0:1, 2:3]
        self.pb = [self.nc.alloc_psum_tensor(f"pb{i}", [128, 512], F32) for i in range(6)]
        self.rep_banks = (2, 3)
        self.pbb = [self.nc.alloc_psum_tensor(f"pbb{i}", [128, 1024], BF16) for i in range(2)]

    def uid(self):
        self._uid += 1
        return self._uid

    def inp(self, name, shape, dt=F32):
        return self.nc.dram_tensor(name, shape, dt, kind="ExternalInput").ap()

    def outp(self, name, shape, dt=F32):
        return self.nc.dram_tensor(name, shape, dt, kind="ExternalOutput").ap()

    def sb(self, name, p, n, dt=F32):
        return self.nc.alloc_sbuf_tensor(name, [p, n], dt)

    def sb_shared(self, key, p, n, dt=F32):
        if not hasattr(self, "_shared"):
            self._shared = {}
        k = (key, p, n, dt)
        if k not in self._shared:
            self._shared[k] = self.nc.alloc_sbuf_tensor(key, [p, n], dt)
        return self._shared[k]

def emit_ln(ctx, x, NT, xn, tag, gamma_beta=None, second_ln=False):
    pr = ctx.pr
    nblk = NT // 512
    m_row = ctx.sb_shared(f"{tag}_m", 1, NT)
    v_row = ctx.sb_shared(f"{tag}_v", 1, NT)
    var_row = ctx.sb_shared(f"{tag}_var", 1, NT)
    rstd_row = ctx.sb_shared(f"{tag}_rs", 1, NT)
    mr_row = ctx.sb_shared(f"{tag}_mr", 1, NT)
    sq = ctx.sb_shared(f"{tag}_sq", 128, KT * NT, BF16)
    t1a = ctx.sb_shared(f"{tag}_t1a", 128, 512)
    t1b = ctx.sb_shared(f"{tag}_t1b", 128, 512)

    pr.step()
    sqt = []
    for k in range(KT):
        sqt.append(pr.op("dve", "tensor_mul", sq[:, k * NT:(k + 1) * NT],
                         x[:, k * NT:(k + 1) * NT], x[:, k * NT:(k + 1) * NT]))
    pm, pv = ctx.pb[0][0:1, :], ctx.pb[1][0:1, :]
    prev = [None, None]
    for blk in range(nblk):
        mm = mv = None
        for k in range(KT):
            mm = pr.op("pe", "matmul", pm, ctx.inv_col,
                       x[:, k * NT + blk * 512: k * NT + blk * 512 + 512],
                       start=(k == 0), stop=(k == KT - 1),
                       deps=([prev[0]] if k == 0 else []))
            mv = pr.op("pe", "matmul", pv, ctx.invb_col,
                       sq[:, k * NT + blk * 512: k * NT + blk * 512 + 512],
                       start=(k == 0), stop=(k == KT - 1),
                       deps=[sqt[k]] + ([prev[1]] if k == 0 else []))
        e1 = pr.op("dve", "tensor_copy", m_row[:, blk * 512:blk * 512 + 512], pm, deps=[mm])
        e2 = pr.op("dve", "tensor_copy", v_row[:, blk * 512:blk * 512 + 512], pv, deps=[mv])
        prev = [e1, e2]
    pr.step()
    pr.op("dve", "tensor_mul", var_row[:, :], m_row[:, :], m_row[:, :])
    pr.step()
    pr.op("dve", "tensor_sub", var_row[:, :], v_row[:, :], var_row[:, :])
    pr.step()
    pr.op("act", "activation", rstd_row[:, :], var_row[:, :], AF.Ln, bias=ctx.eps11)
    pr.op("act", "activation", rstd_row[:, :], rstd_row[:, :], AF.Exp, scale=-0.5)
    pr.step()
    pr.op("dve", "tensor_mul", mr_row[:, :], m_row[:, :], rstd_row[:, :])
    pr.step()
    prs, prm = ctx.pb[ctx.rep_banks[0]], ctx.pb[ctx.rep_banks[1]]
    prev_rep = None
    for blk in range(nblk):
        d = [prev_rep] if prev_rep else []
        r1 = pr.op("pe", "matmul", prs[:, :], ctx.onesrow[:, :],
                   rstd_row[:, blk * 512: blk * 512 + 512], start=True, stop=True, deps=d)
        r2 = pr.op("pe", "matmul", prm[:, :], ctx.onesrow[:, :],
                   mr_row[:, blk * 512: blk * 512 + 512], start=True, stop=True, deps=d)
        last = None
        for k in range(KT):
            t1 = t1a if k % 2 == 0 else t1b
            sl = slice(k * NT + blk * 512, k * NT + blk * 512 + 512)
            s1 = pr.op("dve", "tensor_mul", t1[:, :], x[:, sl], prs[:, :], deps=[r1])
            if gamma_beta is None:
                last = pr.op("dve", "tensor_sub", xn[:, sl], t1[:, :], prm[:, :], deps=[s1, r2])
            else:
                g, bcols = gamma_beta
                s2 = pr.op("dve", "tensor_sub", t1[:, :], t1[:, :], prm[:, :], deps=[s1, r2])
                last = pr.op("dve", "tensor_scalar", xn[:, sl], t1[:, :],
                             g[k], bcols[k], OP.mult, op1=OP.add, deps=[s2])
        prev_rep = last
    pr.step()
    return m_row, rstd_row

def build_l3f():
    ctx = Ctx()
    pr = ctx.pr
    NT = 1024
    FT = 32
    x_d = ctx.inp("x", [D, TC])
    w1_d = ctx.inp("w1h", [FT, KT, 128, 128])
    b1_d = ctx.inp("b1c", [128, FT])
    w2_d = ctx.inp("w2h", [KT, FT, 128, 128])
    b2_d = ctx.inp("b2c", [128, KT])
    xo_d = ctx.outp("xo", [D, TC])
    if DEBUG_MOE:
        dbg_lg_d = ctx.outp("dbg_lg", [8, NT])
        dbg_wd_d = ctx.outp("dbg_wd", [8, NT], BF16)
        dbg_we_d = ctx.outp("dbg_we", [128, NT])
        dbg_xn_d = ctx.outp("dbg_xn", [128, KT * NT], BF16)
        dbg_rt_d = ctx.outp("dbg_rt", [128, 40])
        dbg_srt_d = ctx.outp("dbg_srt", [128, 8])

    x = ctx.sb("s_x", 128, KT * NT)
    xn = ctx.sb("s_xn", 128, KT * NT, BF16)
    h1 = ctx.sb("s_h1", 128, FT * NT, BF16)
    b1c = ctx.sb("s_b1c", 128, FT)
    b2c = ctx.sb("s_b2c", 128, KT)
    tmp = [ctx.sb(f"tmp{i}", 128, 512) for i in range(2)]
    w1t = [ctx.sb(f"w1t{i}", 128, KT * 128, BF16) for i in range(2)]
    w2t = [ctx.sb(f"w2t{i}", 128, FT * 128, BF16) for i in range(2)]

    lb1 = pr.dma("sp", b1c[:, :], b1_d[:, :])
    lb2 = pr.dma("sp", b2c[:, :], b2_d[:, :])
    pr.step()
    pr.op("dve", "tensor_scalar_mul", b1c[:, :], b1c[:, :], ctx.half_col)
    pr.op("dve", "tensor_scalar_mul", b2c[:, :], b2c[:, :], ctx.half_col)
    pr.step()

    for half in range(2):
        off = half * NT
        xl = [pr.dma("sp", x[:, k * NT:(k + 1) * NT],
                     x_d[k * 128:(k + 1) * 128, off:off + NT]) for k in range(KT)]
        pr.step()
        emit_ln(ctx, x, NT, xn, "ln3f")
        wdma = [None, None]
        wlast = [pr.last["pe"], pr.last["pe"]]
        bankr = {0: None, 1: None}
        for m in range(FT):
            if wdma[m % 2] is None:
                wdma[m % 2] = pr.dma("pool", w1t[m % 2][:, :].rearrange("p (k f) -> p k f", k=KT),
                                     w1_d[m].rearrange("k c f -> c k f"), nobar=True,
                                     deps=[wlast[m % 2]])
            wd = wdma[m % 2]
            for blk in range(NT // 512):
                bank = ctx.pb[4 + (blk % 2)]
                mm = None
                for k in range(KT):
                    mm = pr.op("pe", "matmul", bank[:, :],
                               w1t[m % 2][:, k * 128:(k + 1) * 128],
                               xn[:, k * NT + blk * 512: k * NT + blk * 512 + 512],
                               start=(k == 0), stop=(k == KT - 1),
                               deps=[wd], nobar=(k > 0))
                pr.op("act", "activation",
                      h1[:, m * NT + blk * 512: m * NT + blk * 512 + 512],
                      bank[:, :], AF.Identity, scale=0.5, bias=b1c[:, m:m + 1],
                      deps=[mm], nobar=True)
            wlast[m % 2] = pr.last["pe"]
            if m + 1 < FT:
                wdma[(m + 1) % 2] = pr.dma("pool", w1t[(m + 1) % 2][:, :].rearrange("p (k f) -> p k f", k=KT),
                                           w1_d[m + 1].rearrange("k c f -> c k f"),
                                           nobar=True, deps=[wlast[(m + 1) % 2]])
            else:
                wdma[(m + 1) % 2] = None
        pr.step()
        wdma = [None, None]
        wlast = [pr.last["pe"], pr.last["pe"]]
        bankr = {0: None, 1: None}
        for c in range(KT):
            if wdma[c % 2] is None:
                wdma[c % 2] = pr.dma("pool", w2t[c % 2][:, :].rearrange("p (k c) -> p k c", k=FT),
                                     w2_d[c].rearrange("k f c -> f k c"), nobar=True,
                                     deps=[wlast[c % 2]])
            wd = wdma[c % 2]
            for blk in range(NT // 512):
                bank = ctx.pb[4 + (blk % 2)]
                mm = None
                for kf in range(FT):
                    mm = pr.op("pe", "matmul", bank[:, :],
                               w2t[c % 2][:, kf * 128:(kf + 1) * 128],
                               h1[:, kf * NT + blk * 512: kf * NT + blk * 512 + 512],
                               start=(kf == 0), stop=(kf == FT - 1),
                               deps=[wd] + ([bankr[blk % 2]] if kf == 0 else []),
                               nobar=(kf > 0))
                tb = tmp[blk % 2]
                t1 = pr.op("act", "activation", tb[:, :], bank[:, :], AF.Identity,
                           bias=b2c[:, c:c + 1], deps=[mm], nobar=True)
                bankr[blk % 2] = t1
                sl = slice(c * NT + blk * 512, c * NT + blk * 512 + 512)
                t2 = pr.op("dve", "tensor_add", x[:, sl], x[:, sl], tb[:, :],
                           deps=[t1], nobar=True)
                pr.dma("sp", xo_d[c * 128:(c + 1) * 128, off + blk * 512: off + blk * 512 + 512],
                       x[:, sl], deps=[t2], nobar=True)
            wlast[c % 2] = pr.last["pe"]
            if c + 1 < KT:
                wdma[(c + 1) % 2] = pr.dma("pool", w2t[(c + 1) % 2][:, :].rearrange("p (k c) -> p k c", k=FT),
                                           w2_d[c + 1].rearrange("k f c -> f k c"),
                                           nobar=True, deps=[wlast[(c + 1) % 2]])
            else:
                wdma[(c + 1) % 2] = None
        pr.step()
    pr.finalize()
    return ctx.nc

def prep_l3f(inp, core):
    W1 = inp["L3_ffn_W1"][core]
    b1 = inp["L3_ffn_b1"][core]
    W2 = inp["L3_ffn_W2"][core]
    b2 = inp["L3_ffn_b2"][core]
    w1h = np.ascontiguousarray(
        W1.reshape(32, 128, 8, 128).transpose(0, 2, 3, 1))
    b1c = np.ascontiguousarray(b1.reshape(32, 128).T)
    w2h = np.ascontiguousarray(
        W2.reshape(8, 128, 32, 128).transpose(0, 2, 3, 1))
    b2c = np.ascontiguousarray(b2.reshape(8, 128).T)
    return {"w1h": w1h, "b1c": b1c, "w2h": w2h, "b2c": b2c}

def build_l2f():
    ctx = Ctx()
    pr = ctx.pr
    NT = 1024
    NTH = NT + 1
    x_d = ctx.inp("x", [D, TC])
    xh_d = ctx.inp("xh", [128, KT])
    hm_d = ctx.inp("hmask", [128, 1])
    wr_d = ctx.inp("wrh", [KT, KT, 128, 128])
    wk_d = ctx.inp("wkh", [KT, KT, 128, 128])
    wv_d = ctx.inp("wvh", [KT, KT, 128, 128])
    mur_d = ctx.inp("murc", [128, KT])
    muk_d = ctx.inp("mukc", [128, KT])
    xo_d = ctx.outp("xo", [D, TC])
    if DEBUG_MOE:
        dbg_lg_d = ctx.outp("dbg_lg", [8, NT])
        dbg_wd_d = ctx.outp("dbg_wd", [8, NT], BF16)
        dbg_we_d = ctx.outp("dbg_we", [128, NT])
        dbg_xn_d = ctx.outp("dbg_xn", [128, KT * NT], BF16)
        dbg_rt_d = ctx.outp("dbg_rt", [128, 40])
        dbg_srt_d = ctx.outp("dbg_srt", [128, 8])

    x = ctx.sb("s_x", 128, KT * NT)
    xn = ctx.sb("s_xn", 128, KT * NTH, BF16)
    xr = ctx.sb("s_xr", 128, KT * NT, BF16)
    xk = ctx.sb("s_xk", 128, KT * NT, BF16)
    r = ctx.sb("s_r", 128, KT * NT)
    k2 = ctx.sb("s_k2", 128, KT * NT, BF16)
    xhal = ctx.sb("s_xhal", 128, KT)
    hmask = ctx.sb("s_hm", 128, 1)
    mur = ctx.sb("s_mur", 128, KT)
    muk = ctx.sb("s_muk", 128, KT)
    omur = ctx.sb("s_omur", 128, KT)
    omuk = ctx.sb("s_omuk", 128, KT)
    hrow = ctx.sb("s_hrow", 1, 8)
    hsc = ctx.sb("s_hsc", 1, 4)
    hrep = ctx.sb("s_hrep", 128, 2)
    tmp = [ctx.sb(f"s_tmp{i}", 128, 512) for i in range(2)]
    tmix = ctx.sb("s_tmix", 128, NT, BF16)
    wt = [ctx.sb(f"s_wt{i}", 128, KT * 128, BF16) for i in range(2)]

    pr.dma("sp", xhal[:, :], xh_d[:, :])
    pr.dma("sp", hmask[:, :], hm_d[:, :])
    pr.dma("sp", mur[:, :], mur_d[:, :])
    pr.dma("sp", muk[:, :], muk_d[:, :])
    pr.step()
    pr.op("dve", "tensor_scalar", omur[:, :], mur[:, :], -1.0, 1.0, OP.mult, op1=OP.add)
    pr.op("dve", "tensor_scalar", omuk[:, :], muk[:, :], -1.0, 1.0, OP.mult, op1=OP.add)
    ps0 = ctx.pb[0]
    m1 = pr.op("pe", "matmul", ps0[0:1, 0:KT], ctx.inv_col, xhal[:, :], start=True, stop=True)
    sqh = pr.op("dve", "tensor_mul", hrep[:, 0:1], xhal[:, 0:1], xhal[:, 0:1])
    sqhal = ctx.sb("s_sqhal", 128, KT)
    sq2 = pr.op("dve", "tensor_mul", sqhal[:, :], xhal[:, :], xhal[:, :])
    pr.step()
    m2 = pr.op("pe", "matmul", ps0[0:1, 8:8 + KT], ctx.inv_col, sqhal[:, :], start=True, stop=True)
    pr.step()
    rm = pr.op("dve", "reduce_sum", hsc[:, 0:1], ps0[0:1, 0:KT], axis=AX)
    rv = pr.op("dve", "reduce_sum", hsc[:, 1:2], ps0[0:1, 8:8 + KT], axis=AX)
    pr.step()
    pr.op("pool", "tensor_mul", hsc[:, 2:3], hsc[:, 0:1], hsc[:, 0:1], deps=[rm])
    pr.step()
    pr.op("pool", "tensor_sub", hsc[:, 2:3], hsc[:, 1:2], hsc[:, 2:3], deps=[rv])
    pr.step()
    pr.op("act", "activation", hsc[:, 3:4], hsc[:, 2:3], AF.Ln, bias=ctx.eps11)
    pr.op("act", "activation", hsc[:, 3:4], hsc[:, 3:4], AF.Exp, scale=-0.5)
    pr.step()
    pr.op("pool", "tensor_mul", hsc[:, 0:1], hsc[:, 0:1], hsc[:, 3:4])
    pr.step()
    b1 = pr.op("pe", "matmul", ps0[:, 0:1], ctx.onesrow[:, :], hsc[:, 0:1], start=True, stop=True)
    b2 = pr.op("pe", "matmul", ps0[:, 1:2], ctx.onesrow[:, :], hsc[:, 3:4], start=True, stop=True)
    pr.step()
    pr.op("dve", "tensor_copy", hrep[:, :], ps0[:, 0:2])
    pr.step()
    xnh = ctx.sb("s_xnh", 128, KT)
    pr.op("dve", "scalar_tensor_tensor", xnh[:, :], xhal[:, :], hrep[:, 1:2],
          hrep[:, 0:1].broadcast_to((128, KT)), OP.mult, OP.subtract)
    pr.step()
    pr.op("dve", "tensor_scalar_mul", xnh[:, :], xnh[:, :], hmask[:, 0:1])
    pr.step()

    for half in range(2):
        off = half * NT
        for k in range(KT):
            pr.dma("sp", x[:, k * NT:(k + 1) * NT],
                   x_d[k * 128:(k + 1) * 128, off:off + NT])
        if half == 0:
            for k in range(KT):
                pr.op("dve", "tensor_copy", xn[:, k * NTH:k * NTH + 1], xnh[:, k:k + 1])
        else:
            for k in range(KT):
                pr.op("dve", "tensor_copy", xn[:, k * NTH:k * NTH + 1],
                      xn[:, k * NTH + NT:k * NTH + NT + 1])
        pr.step()
        emit_ln_strided(ctx, x, NT, xn, NTH, 1, "ln2f")
        for k in range(KT):
            t1 = pr.op("dve", "tensor_scalar_mul", tmix[:, 0:NT],
                       xn[:, k * NTH + 1:k * NTH + 1 + NT], mur[:, k:k + 1])
            pr.op("dve", "scalar_tensor_tensor", xr[:, k * NT:(k + 1) * NT],
                  xn[:, k * NTH:k * NTH + NT], omur[:, k:k + 1], tmix[:, 0:NT],
                  OP.mult, OP.add, deps=[t1])
            t2 = pr.op("dve", "tensor_scalar_mul", tmix[:, 0:NT],
                       xn[:, k * NTH + 1:k * NTH + 1 + NT], muk[:, k:k + 1])
            pr.op("dve", "scalar_tensor_tensor", xk[:, k * NT:(k + 1) * NT],
                  xn[:, k * NTH:k * NTH + NT], omuk[:, k:k + 1], tmix[:, 0:NT],
                  OP.mult, OP.add, deps=[t2])
        pr.step()
        emit_sq_proj(ctx, wr_d, xr, r, NT, act="sigmoid")
        emit_sq_proj(ctx, wk_d, xk, k2, NT, act="relusq")
        wdma = [None, None]
        wlast = [pr.last["pe"], pr.last["pe"]]
        bankr = {0: None, 1: None}
        for m in range(KT):
            if wdma[m % 2] is None:
                wdma[m % 2] = pr.dma("pool", wt[m % 2][:, :].rearrange("p (k o) -> p k o", k=KT),
                                     wv_d[m].rearrange("k c o -> c k o"), nobar=True,
                                     deps=[wlast[m % 2]])
            wd = wdma[m % 2]
            for blk in range(NT // 512):
                bank = ctx.pb[4 + (blk % 2)]
                mm = None
                for k in range(KT):
                    mm = pr.op("pe", "matmul", bank[:, :],
                               wt[m % 2][:, k * 128:(k + 1) * 128],
                               k2[:, k * NT + blk * 512: k * NT + blk * 512 + 512],
                               start=(k == 0), stop=(k == KT - 1),
                               deps=[wd] + ([bankr[blk % 2]] if k == 0 else []),
                               nobar=(k > 0))
                tb = tmp[blk % 2]
                sl = slice(m * NT + blk * 512, m * NT + blk * 512 + 512)
                t1 = pr.op("dve", "tensor_mul", tb[:, :], r[:, sl], bank[:, :],
                           deps=[mm], nobar=True)
                bankr[blk % 2] = t1
                t2 = pr.op("dve", "scalar_tensor_tensor", x[:, sl], tb[:, :],
                           ctx.half_col, x[:, sl], OP.mult, OP.add, deps=[t1], nobar=True)
                pr.dma("sp", xo_d[m * 128:(m + 1) * 128, off + blk * 512: off + blk * 512 + 512],
                       x[:, sl], deps=[t2], nobar=True)
            wlast[m % 2] = pr.last["pe"]
            if m + 1 < KT:
                wdma[(m + 1) % 2] = pr.dma("pool", wt[(m + 1) % 2][:, :].rearrange("p (k o) -> p k o", k=KT),
                                           wv_d[m + 1].rearrange("k c o -> c k o"),
                                           nobar=True, deps=[wlast[(m + 1) % 2]])
            else:
                wdma[(m + 1) % 2] = None
        pr.step()
    pr.finalize()
    return ctx.nc

def emit_sq_proj(ctx, w_d, src, dst, NT, act):
    pr = ctx.pr
    wt = [ctx.sb_shared(f"sqp_w{i}", 128, KT * 128, BF16) for i in range(2)]
    rel = ctx.sb_shared("sqp_rel", 128, 512)
    wdma = [None, None]
    wlast = [pr.last["pe"], pr.last["pe"]]
    bankr = {0: None, 1: None}
    for m in range(KT):
        if wdma[m % 2] is None:
            wdma[m % 2] = pr.dma("pool", wt[m % 2][:, :].rearrange("p (k o) -> p k o", k=KT),
                                 w_d[m].rearrange("k c o -> c k o"), nobar=True,
                                 deps=[wlast[m % 2]])
        wd = wdma[m % 2]
        for blk in range(NT // 512):
            bank = ctx.pb[4 + (blk % 2)]
            mm = None
            for k in range(KT):
                mm = pr.op("pe", "matmul", bank[:, :],
                           wt[m % 2][:, k * 128:(k + 1) * 128],
                           src[:, k * NT + blk * 512: k * NT + blk * 512 + 512],
                           start=(k == 0), stop=(k == KT - 1),
                           deps=[wd] + ([bankr[blk % 2]] if k == 0 else []),
                           nobar=(k > 0))
            sl = slice(m * NT + blk * 512, m * NT + blk * 512 + 512)
            if act == "sigmoid":
                bankr[blk % 2] = pr.op("act", "activation", dst[:, sl], bank[:, :], AF.Sigmoid,
                      deps=[mm], nobar=True)
            else:
                t1 = pr.op("act", "activation", rel[:, :], bank[:, :], AF.Relu,
                           deps=[mm], nobar=True)
                bankr[blk % 2] = t1
                pr.op("act", "activation", dst[:, sl], rel[:, :], AF.Square,
                      deps=[t1], nobar=True)
        wlast[m % 2] = pr.last["pe"]
        if m + 1 < KT:
            wdma[(m + 1) % 2] = pr.dma("pool", wt[(m + 1) % 2][:, :].rearrange("p (k o) -> p k o", k=KT),
                                       w_d[m + 1].rearrange("k c o -> c k o"),
                                       nobar=True, deps=[wlast[(m + 1) % 2]])
        else:
            wdma[(m + 1) % 2] = None
    pr.step()

def emit_ln_strided(ctx, x, NT, xn, NTH, xn_off, tag):
    pr = ctx.pr
    nblk = NT // 512
    m_row = ctx.sb_shared(f"{tag}_m", 1, NT)
    v_row = ctx.sb_shared(f"{tag}_v", 1, NT)
    var_row = ctx.sb_shared(f"{tag}_var", 1, NT)
    rstd_row = ctx.sb_shared(f"{tag}_rs", 1, NT)
    mr_row = ctx.sb_shared(f"{tag}_mr", 1, NT)
    sq = ctx.sb_shared(f"{tag}_sq", 128, KT * NT, BF16)
    t1a = ctx.sb_shared(f"{tag}_t1a", 128, 512)
    t1b = ctx.sb_shared(f"{tag}_t1b", 128, 512)
    pr.step()
    sqt = []
    for k in range(KT):
        sqt.append(pr.op("dve", "tensor_mul", sq[:, k * NT:(k + 1) * NT],
                         x[:, k * NT:(k + 1) * NT], x[:, k * NT:(k + 1) * NT]))
    pm, pv = ctx.pb[0][0:1, :], ctx.pb[1][0:1, :]
    prev = [None, None]
    for blk in range(nblk):
        mm = mv = None
        for k in range(KT):
            mm = pr.op("pe", "matmul", pm, ctx.inv_col,
                       x[:, k * NT + blk * 512: k * NT + blk * 512 + 512],
                       start=(k == 0), stop=(k == KT - 1),
                       deps=([prev[0]] if k == 0 else []))
            mv = pr.op("pe", "matmul", pv, ctx.invb_col,
                       sq[:, k * NT + blk * 512: k * NT + blk * 512 + 512],
                       start=(k == 0), stop=(k == KT - 1),
                       deps=[sqt[k]] + ([prev[1]] if k == 0 else []))
        e1 = pr.op("dve", "tensor_copy", m_row[:, blk * 512:blk * 512 + 512], pm, deps=[mm])
        e2 = pr.op("dve", "tensor_copy", v_row[:, blk * 512:blk * 512 + 512], pv, deps=[mv])
        prev = [e1, e2]
    pr.step()
    pr.op("dve", "tensor_mul", var_row[:, :], m_row[:, :], m_row[:, :])
    pr.step()
    pr.op("dve", "tensor_sub", var_row[:, :], v_row[:, :], var_row[:, :])
    pr.step()
    pr.op("act", "activation", rstd_row[:, :], var_row[:, :], AF.Ln, bias=ctx.eps11)
    pr.op("act", "activation", rstd_row[:, :], rstd_row[:, :], AF.Exp, scale=-0.5)
    pr.step()
    pr.op("dve", "tensor_mul", mr_row[:, :], m_row[:, :], rstd_row[:, :])
    pr.step()
    prs, prm = ctx.pb[ctx.rep_banks[0]], ctx.pb[ctx.rep_banks[1]]
    prev_rep = None
    for blk in range(nblk):
        d = [prev_rep] if prev_rep else []
        r1 = pr.op("pe", "matmul", prs[:, :], ctx.onesrow[:, :],
                   rstd_row[:, blk * 512: blk * 512 + 512], start=True, stop=True, deps=d)
        r2 = pr.op("pe", "matmul", prm[:, :], ctx.onesrow[:, :],
                   mr_row[:, blk * 512: blk * 512 + 512], start=True, stop=True, deps=d)
        last = None
        for k in range(KT):
            t1 = t1a if k % 2 == 0 else t1b
            s1 = pr.op("dve", "tensor_mul", t1[:, :],
                       x[:, k * NT + blk * 512: k * NT + blk * 512 + 512], prs[:, :], deps=[r1])
            last = pr.op("dve", "tensor_sub",
                         xn[:, k * NTH + xn_off + blk * 512: k * NTH + xn_off + blk * 512 + 512],
                         t1[:, :], prm[:, :], deps=[s1, r2])
        prev_rep = last
    pr.step()

def prep_l2f(inp, core):
    Wr = inp["L2_ffn_W_r"]; Wk = inp["L2_ffn_W_k"]; Wv = inp["L2_ffn_W_v"]
    mur = inp["L2_ffn_mu_r"]; muk = inp["L2_ffn_mu_k"]
    def wh(W):
        return np.ascontiguousarray(W.reshape(KT, 128, KT, 128).transpose(0, 2, 3, 1))
    hm = np.ones((128, 1), np.float32) if core > 0 else np.zeros((128, 1), np.float32)
    return {"wrh": wh(Wr), "wkh": wh(Wk), "wvh": wh(Wv),
            "murc": np.ascontiguousarray(mur.reshape(KT, 128).T),
            "mukc": np.ascontiguousarray(muk.reshape(KT, 128).T),
            "hmask": hm}

DEBUG_MOE = False

def build_moe(layer):
    assert layer in (0, 1)
    FT = 16 if layer == 0 else 32
    ctx = Ctx()
    pr = ctx.pr
    NT = 1024
    x_d = ctx.inp("x", [D, TC])
    gw_d = ctx.inp("gwh", [KT, 128, 8])
    w1_d = ctx.inp("w1h", [8, FT, KT, 128, 128])
    w2_d = ctx.inp("w2h", [8, KT, FT, 128, 128])
    sel_d = ctx.inp("sel8h", [8, 8, 128])
    if layer == 1:
        gam_d = ctx.inp("gamc", [128, KT])
        bet_d = ctx.inp("betc", [128, KT])
    xo_d = ctx.outp("xo", [D, TC])
    if DEBUG_MOE:
        dbg_lg_d = ctx.outp("dbg_lg", [8, NT])
        dbg_wd_d = ctx.outp("dbg_wd", [8, NT], BF16)
        dbg_we_d = ctx.outp("dbg_we", [128, NT])
        dbg_xn_d = ctx.outp("dbg_xn", [128, KT * NT], BF16)
        dbg_rt_d = ctx.outp("dbg_rt", [128, 40])
        dbg_srt_d = ctx.outp("dbg_srt", [128, 8])

    x = ctx.sb("s_x", 128, KT * NT)
    xn = ctx.sb("s_xn", 128, KT * NT, BF16)
    if layer == 1:
        xs = ctx.sb("s_xs", 128, KT * NT, BF16)
        gam = ctx.sb("s_gam", 128, KT)
        bet = ctx.sb("s_bet", 128, KT)
        pr.dma("sp", gam[:, :], gam_d[:, :])
        pr.dma("sp", bet[:, :], bet_d[:, :])
    else:
        xs = xn
    hb = ctx.sb("s_hb", 128, FT * NT, BF16)
    gw = ctx.sb("s_gw", 128, KT * 8, BF16)
    sel8 = ctx.sb("s_sel8", 8, 8 * 128, BF16)
    logits = ctx.sb("s_lg", 8, NT)
    wd_em = ctx.sb("s_wdem", 8, NT, BF16)
    werep = ctx.sb("s_werep", 128, NT)
    rt = ctx.sb("s_rt", 128, 40)
    msk8 = ctx.sb("s_msk8", 128, 8, U8)
    srt = ctx.sb("s_srt", 128, 8)
    w1t = [ctx.sb(f"s_w1t{i}", 128, KT * 128, BF16) for i in range(2)]
    w2t = [ctx.sb(f"s_w2t{i}", 128, FT * 128, BF16) for i in range(2)]
    tmp = [ctx.sb(f"s_tmp{i}", 128, 512) for i in range(2)]

    pr.dma("pool", gw[:, :].rearrange("p (k e) -> p k e", k=KT), gw_d.rearrange("k p e -> p k e"))
    pr.dma("pool", sel8[:, :].rearrange("p (e c) -> p e c", e=8), sel_d.rearrange("e k c -> k e c"))
    pr.step()

    for half in range(2):
        off = half * NT
        for k in range(KT):
            pr.dma("sp", x[:, k * NT:(k + 1) * NT],
                   x_d[k * 128:(k + 1) * 128, off:off + NT])
        pr.step()
        emit_ln(ctx, x, NT, xn, "lnm")
        if layer == 1:
            emit_ln2_affine(ctx, xn, NT, xs, gam, bet, "lnm")
        pbl = ctx.pb[0]
        for blk in range(NT // 512):
            mm = None
            for k in range(KT):
                mm = pr.op("pe", "matmul", pbl[0:8, 0:512], gw[:, k * 8:(k + 1) * 8].rearrange("p e -> p e") if False else gw[:, k * 8:(k + 1) * 8],
                           xs[:, k * NT + blk * 512:k * NT + blk * 512 + 512],
                           start=(k == 0), stop=(k == KT - 1),
                           deps=([prevl] if (k == 0 and blk > 0) else []))
            prevl = pr.op("dve", "tensor_copy", logits[:, blk * 512:blk * 512 + 512],
                          pbl[0:8, 0:512], deps=[mm])
        pr.step()
        if DEBUG_MOE and half == 0:
            pr.dma("sp", dbg_lg_d[:, :], logits[:, :])
            pr.step()
        pbt = ctx.pb[1]
        prevw = None
        for ch in range(NT // 128):
            tt = pr.op("pe", "transpose", pbt[:, 0:8], logits[:, ch * 128:(ch + 1) * 128],
                       ctx.eye[0:8, 0:8], deps=([prevw] if prevw else []))
            c0 = pr.op("act", "copy", rt[:, 0:8], pbt[:, 0:8], deps=[tt])
            if layer == 0:
                m1 = pr.op("dve", "reduce_max", srt[:, 0:1], rt[:, 0:8], axis=AX, deps=[c0])
                e1 = pr.op("pool", "tensor_single_scalar", msk8[:, :], rt[:, 0:8], srt[:, 0:1], OP.is_equal, deps=[m1, c0])
                mk = pr.op("dve", "select", rt[:, 24:32], msk8[:, :],
                           ctx.neg_col.broadcast_to((128, 8)), rt[:, 0:8], deps=[e1])
                mkb = pr.op("pool", "tensor_copy", rt[:, 32:40], rt[:, 24:32], deps=[mk])
                m2 = pr.op("dve", "reduce_max", srt[:, 1:2], rt[:, 32:40], axis=AX, deps=[mkb])
                s1 = pr.op("pool", "tensor_sub", rt[:, 8:9], srt[:, 0:1], srt[:, 1:2], deps=[m1, m2])
                s2 = pr.op("act", "activation", rt[:, 9:10], rt[:, 8:9], AF.Sigmoid, deps=[s1])
                s3 = pr.op("pool", "tensor_scalar", rt[:, 10:11], rt[:, 9:10], -1.0, 1.0,
                           OP.mult, op1=OP.add, deps=[s2])
                w1m = pr.op("dve", "select", rt[:, 16:24], msk8[:, :],
                            rt[:, 9:10].broadcast_to((128, 8)), ctx.zeros8[:, :], deps=[s2, e1])
                e2 = pr.op("pool", "tensor_single_scalar", msk8[:, :], rt[:, 0:8], srt[:, 1:2], OP.is_equal, deps=[w1m, m2])
                w2m = pr.op("dve", "select", rt[:, 24:32], msk8[:, :],
                            rt[:, 10:11].broadcast_to((128, 8)), ctx.zeros8[:, :], deps=[e2, s3])
                wd = pr.op("pool", "tensor_add", rt[:, 11:19], rt[:, 16:24], rt[:, 24:32], deps=[w1m, w2m])
            else:
                mx = pr.op("dve", "reduce_max", rt[:, 8:9], rt[:, 0:8], axis=AX, deps=[c0])
                bc = pr.op("pool", "tensor_scalar_mul", rt[:, 9:10], rt[:, 8:9], -1.0 / 1.5, deps=[mx])
                ex = pr.op("act", "activation", rt[:, 11:19], rt[:, 0:8], AF.Exp,
                           scale=1.0 / 1.5, bias=rt[:, 9:10], deps=[bc])
                sm = pr.op("dve", "reduce_sum", rt[:, 8:9], rt[:, 11:19], axis=AX, deps=[ex])
                smb = pr.op("pool", "tensor_copy", rt[:, 32:33], rt[:, 8:9], deps=[sm])
                rc = pr.op("dve", "reciprocal", rt[:, 9:10], rt[:, 32:33], deps=[smb])
                prb = pr.op("pool", "tensor_scalar_mul", rt[:, 11:19], rt[:, 11:19], rt[:, 9:10], deps=[rc, ex])
                m1 = pr.op("dve", "reduce_max", srt[:, 0:1], rt[:, 11:19], axis=AX, deps=[prb])
                e1 = pr.op("pool", "tensor_single_scalar", msk8[:, :], rt[:, 11:19], srt[:, 0:1], OP.is_equal, deps=[m1, prb])
                mk = pr.op("dve", "select", rt[:, 24:32], msk8[:, :],
                           ctx.neg_col.broadcast_to((128, 8)), rt[:, 11:19], deps=[e1])
                mkb = pr.op("pool", "tensor_copy", rt[:, 32:40], rt[:, 24:32], deps=[mk])
                m2 = pr.op("dve", "reduce_max", srt[:, 1:2], rt[:, 32:40], axis=AX, deps=[mkb])
                ge = pr.op("pool", "tensor_single_scalar", msk8[:, :], rt[:, 11:19], srt[:, 1:2], OP.is_ge, deps=[m2, mk])
                wd = pr.op("dve", "select", rt[:, 11:19], msk8[:, :], rt[:, 11:19], ctx.zeros8[:, :], deps=[ge, prb])
            if DEBUG_MOE and half == 0 and ch == 0:
                pr.dma("sp", dbg_rt_d[:, :], rt[:, :], deps=[wd])
                pr.dma("sp", dbg_srt_d[:, :], srt[:, :], deps=[wd])
            ttb = pr.op("pe", "transpose", pbt[:, 8:16] if False else pbt[0:8, 8:136],
                        rt[:, 11:19], ctx.eye[:, :], deps=[wd])
            prevw = pr.op("dve", "tensor_copy", wd_em[:, ch * 128:(ch + 1) * 128],
                          pbt[0:8, 8:136], deps=[ttb])
            pr.step()
        pr.step()
        if DEBUG_MOE and half == 0:
            pr.dma("sp", dbg_wd_d[:, :], wd_em[:, :])
            pr.dma("sp", dbg_xn_d[:, :], xs[:, :])
            pr.step()
        w1dma = [None, None]; w1last = [pr.last["pe"], pr.last["pe"]]
        w2dma = [None, None]; w2last = [pr.last["pe"], pr.last["pe"]]
        hbank = {0: None, 1: None}
        ybank = {0: None, 1: None}
        for e in range(8):
            pbw = ctx.pb[1]
            prevwe = None
            for blk in range(NT // 512):
                wm = pr.op("pe", "matmul", pbw[:, 0:512], sel8[:, e * 128:(e + 1) * 128],
                           wd_em[:, blk * 512:blk * 512 + 512], start=True, stop=True,
                           deps=([prevwe] if prevwe else []))
                prevwe = pr.op("dve", "tensor_copy", werep[:, blk * 512:blk * 512 + 512],
                               pbw[:, 0:512], deps=[wm])
            pr.step()
            if DEBUG_MOE and half == 0 and e == 3:
                pr.dma("sp", dbg_we_d[:, :], werep[:, :])
                pr.step()
            hsil = [[None, None] for _ in range(FT)]
            for m in range(FT):
                i = m % 2
                if w1dma[i] is None:
                    w1dma[i] = pr.dma("pool", w1t[i][:, :].rearrange("p (k f) -> p k f", k=KT),
                                      w1_d[e, m].rearrange("k c f -> c k f"), nobar=True,
                                      deps=[w1last[i]])
                wd1 = w1dma[i]
                for blk in range(NT // 512):
                    bank = ctx.pb[4 + (blk % 2)]
                    mm = None
                    for k in range(KT):
                        mm = pr.op("pe", "matmul", bank[:, :],
                                   w1t[i][:, k * 128:(k + 1) * 128],
                                   xs[:, k * NT + blk * 512: k * NT + blk * 512 + 512],
                                   start=(k == 0), stop=(k == KT - 1),
                                   deps=[wd1] + ([hbank[blk % 2]] if k == 0 else []),
                                   nobar=(k > 0))
                    hbank[blk % 2] = pr.op("act", "activation",
                          hb[:, m * NT + blk * 512: m * NT + blk * 512 + 512],
                          bank[:, :], AF.Silu, deps=[mm], nobar=True)
                    hsil[m][blk] = hbank[blk % 2]
                w1last[i] = pr.last["pe"]
                nm, ne = (m + 1, e) if m + 1 < FT else (0, e + 1)
                if ne < 8:
                    w1dma[(m + 1) % 2] = pr.dma("pool",
                        w1t[(m + 1) % 2][:, :].rearrange("p (k f) -> p k f", k=KT),
                        w1_d[ne, nm].rearrange("k c f -> c k f"), nobar=True,
                        deps=[w1last[(m + 1) % 2]])
                else:
                    w1dma[(m + 1) % 2] = None
            for c in range(KT):
                i = c % 2
                if w2dma[i] is None:
                    w2dma[i] = pr.dma("pool", w2t[i][:, :].rearrange("p (k c) -> p k c", k=FT),
                                      w2_d[e, c].rearrange("k f c -> f k c"), nobar=True,
                                      deps=[w2last[i]])
                wd2 = w2dma[i]
                for blk in range(NT // 512):
                    bank = ctx.pb[2 + (blk % 2)]
                    mm = None
                    for kf in range(FT):
                        mm = pr.op("pe", "matmul", bank[:, :],
                                   w2t[i][:, kf * 128:(kf + 1) * 128],
                                   hb[:, kf * NT + blk * 512: kf * NT + blk * 512 + 512],
                                   start=(kf == 0), stop=(kf == FT - 1),
                                   deps=[wd2, hsil[kf][blk]] + ([ybank[blk % 2]] if kf == 0 else []),
                                   nobar=(kf > 0))
                    tb = tmp[blk % 2]
                    sl = slice(c * NT + blk * 512, c * NT + blk * 512 + 512)
                    t1 = pr.op("dve", "tensor_mul", tb[:, :], werep[:, blk * 512:blk * 512 + 512],
                               bank[:, :], deps=[mm], nobar=True)
                    ybank[blk % 2] = t1
                    pr.op("dve", "tensor_add", x[:, sl], x[:, sl], tb[:, :],
                          deps=[t1], nobar=True)
                w2last[i] = pr.last["pe"]
                nc_, ne = (c + 1, e) if c + 1 < KT else (0, e + 1)
                if ne < 8:
                    w2dma[(c + 1) % 2] = pr.dma("pool",
                        w2t[(c + 1) % 2][:, :].rearrange("p (k c) -> p k c", k=FT),
                        w2_d[ne, nc_].rearrange("k f c -> f k c"), nobar=True,
                        deps=[w2last[(c + 1) % 2]])
                else:
                    w2dma[(c + 1) % 2] = None
            pr.step()
        if layer == 1:
            for k in range(KT):
                for blk in range(NT // 512):
                    sl = slice(k * NT + blk * 512, k * NT + blk * 512 + 512)
                    pr.op("dve", "scalar_tensor_tensor", x[:, sl], xn[:, sl],
                          ctx.half_col, x[:, sl], OP.mult, OP.add)
            pr.step()
        for k in range(KT):
            pr.dma("sp", xo_d[k * 128:(k + 1) * 128, off:off + NT], x[:, k * NT:(k + 1) * NT])
        pr.step()
    pr.finalize()
    return ctx.nc

def emit_ln2_affine(ctx, xn, NT, xs, gam, bet, tag):
    pr = ctx.pr
    nblk = NT // 512
    m_row = ctx.sb_shared(f"{tag}_m", 1, NT)
    v_row = ctx.sb_shared(f"{tag}_v", 1, NT)
    var_row = ctx.sb_shared(f"{tag}_var", 1, NT)
    rstd_row = ctx.sb_shared(f"{tag}_rs", 1, NT)
    mr_row = ctx.sb_shared(f"{tag}_mr", 1, NT)
    sq = ctx.sb_shared(f"{tag}_sq", 128, KT * NT, BF16)
    t1a = ctx.sb_shared(f"{tag}_t1a", 128, 512)
    t1b = ctx.sb_shared(f"{tag}_t1b", 128, 512)
    pr.step()
    sqt = []
    for k in range(KT):
        sqt.append(pr.op("dve", "tensor_mul", sq[:, k * NT:(k + 1) * NT],
                         xn[:, k * NT:(k + 1) * NT], xn[:, k * NT:(k + 1) * NT]))
    pm, pv = ctx.pb[0][0:1, :], ctx.pb[1][0:1, :]
    prev = [None, None]
    for blk in range(nblk):
        mm = mv = None
        for k in range(KT):
            mm = pr.op("pe", "matmul", pm, ctx.invb_col,
                       xn[:, k * NT + blk * 512: k * NT + blk * 512 + 512],
                       start=(k == 0), stop=(k == KT - 1),
                       deps=([prev[0]] if k == 0 else []))
            mv = pr.op("pe", "matmul", pv, ctx.invb_col,
                       sq[:, k * NT + blk * 512: k * NT + blk * 512 + 512],
                       start=(k == 0), stop=(k == KT - 1),
                       deps=[sqt[k]] + ([prev[1]] if k == 0 else []))
        e1 = pr.op("dve", "tensor_copy", m_row[:, blk * 512:blk * 512 + 512], pm, deps=[mm])
        e2 = pr.op("dve", "tensor_copy", v_row[:, blk * 512:blk * 512 + 512], pv, deps=[mv])
        prev = [e1, e2]
    pr.step()
    pr.op("dve", "tensor_mul", var_row[:, :], m_row[:, :], m_row[:, :])
    pr.step()
    pr.op("dve", "tensor_sub", var_row[:, :], v_row[:, :], var_row[:, :])
    pr.step()
    pr.op("act", "activation", rstd_row[:, :], var_row[:, :], AF.Ln, bias=ctx.eps11)
    pr.op("act", "activation", rstd_row[:, :], rstd_row[:, :], AF.Exp, scale=-0.5)
    pr.step()
    pr.op("dve", "tensor_mul", mr_row[:, :], m_row[:, :], rstd_row[:, :])
    pr.step()
    prs, prm = ctx.pb[ctx.rep_banks[0]], ctx.pb[ctx.rep_banks[1]]
    prev_rep = None
    for blk in range(nblk):
        d = [prev_rep] if prev_rep else []
        r1 = pr.op("pe", "matmul", prs[:, :], ctx.onesrow[:, :],
                   rstd_row[:, blk * 512: blk * 512 + 512], start=True, stop=True, deps=d)
        r2 = pr.op("pe", "matmul", prm[:, :], ctx.onesrow[:, :],
                   mr_row[:, blk * 512: blk * 512 + 512], start=True, stop=True, deps=d)
        last = None
        for k in range(KT):
            t1 = t1a if k % 2 == 0 else t1b
            sl = slice(k * NT + blk * 512, k * NT + blk * 512 + 512)
            s1 = pr.op("dve", "tensor_mul", t1[:, :], xn[:, sl], prs[:, :], deps=[r1])
            s2 = pr.op("dve", "tensor_sub", t1[:, :], t1[:, :], prm[:, :], deps=[s1, r2])
            last = pr.op("dve", "tensor_scalar", xs[:, sl], t1[:, :],
                         gam[:, k:k + 1], bet[:, k:k + 1], OP.mult, op1=OP.add, deps=[s2])
        prev_rep = last
    pr.step()

def prep_moe(inp, layer):
    if layer == 0:
        gw = inp["L0_ffn_gate_w"]
        W1 = inp["L0_ffn_w1"]
        W2 = inp["L0_ffn_w2"]
        FT = 16
        extra = {}
    else:
        gw = np.ascontiguousarray(inp["L1_ffn_router_w"].T)
        W1 = inp["L1_ffn_W1"]
        W2 = inp["L1_ffn_W2"]
        FT = 32
        extra = {"gamc": np.ascontiguousarray(inp["L1_ffn_gamma"].reshape(KT, 128).T),
                 "betc": np.ascontiguousarray(inp["L1_ffn_beta"].reshape(KT, 128).T)}
    gwh = np.ascontiguousarray(gw.reshape(KT, 128, 8))
    w1h = np.ascontiguousarray(
        W1.reshape(8, KT, 128, FT, 128).transpose(0, 3, 1, 2, 4))
    w2h = np.ascontiguousarray(
        W2.reshape(8, FT, 128, KT, 128).transpose(0, 3, 1, 2, 4))
    sel = np.zeros((8, 8, 128), np.float32)
    for e in range(8):
        sel[e, e, :] = 0.5
    return dict(gwh=gwh, w1h=w1h, w2h=w2h, sel8h=sel, **extra)

def _l1a_common(ctx, pr, NT):
    names = {}
    names["x_d"] = ctx.inp("x", [D, TC])
    names["xh_d"] = ctx.inp("xh", [128, KT])
    names["hm_d"] = ctx.inp("hmask", [128, 1])
    names["wk_d"] = ctx.inp("wkh", [KT, KT, 128, 128])
    names["wv_d"] = ctx.inp("wvh", [KT, KT, 128, 128])
    names["td_d"] = ctx.inp("tdc", [128, KT])
    names["tmx_d"] = ctx.inp("tmx", [128, 3 * KT])
    return names

def _emit_l1a_prelude(ctx, nm, NT):
    pr = ctx.pr
    h = {}
    h["x"] = ctx.sb("s_x", 128, KT * NT)
    h["xn"] = ctx.sb("s_xn", 128, KT * (NT + 1), BF16)
    h["xk"] = ctx.sb("s_xk", 128, KT * NT, BF16)
    h["xv"] = ctx.sb("s_xv", 128, KT * NT, BF16)
    h["xhal"] = ctx.sb("s_xhal", 128, KT)
    h["hmask"] = ctx.sb("s_hm", 128, 1)
    h["tmx"] = ctx.sb("s_tmx", 128, 3 * KT)
    h["otmx"] = ctx.sb("s_otmx", 128, 3 * KT)
    h["td"] = ctx.sb("s_td", 128, KT)
    h["w"] = ctx.sb("s_w", 128, KT)
    h["wrow"] = ctx.sb("s_wrow", 1, 128 * KT)
    h["lam"] = ctx.sb("s_lam", 128, KT)
    h["tmix"] = ctx.sb("s_tmix", 128, NT, BF16)
    h["hrow"] = ctx.sb("s_hrow", 1, 8)
    h["hsc"] = ctx.sb("s_hsc", 1, 4)
    h["hrep"] = ctx.sb("s_hrep", 128, 2)
    h["xnh"] = ctx.sb("s_xnh", 128, KT)
    h["sqhal"] = ctx.sb("s_sqhal", 128, KT)
    pr.dma("sp", h["xhal"][:, :], nm["xh_d"][:, :])
    pr.dma("sp", h["hmask"][:, :], nm["hm_d"][:, :])
    pr.dma("sp", h["tmx"][:, :], nm["tmx_d"][:, :])
    pr.dma("sp", h["td"][:, :], nm["td_d"][:, :])
    pr.step()
    pr.op("dve", "tensor_scalar", h["otmx"][:, :], h["tmx"][:, :], -1.0, 1.0, OP.mult, op1=OP.add)
    pr.op("act", "activation", h["w"][:, :], h["td"][:, :], AF.Exp)
    pr.step()
    pr.op("dve", "tensor_scalar_mul", h["w"][:, :], h["w"][:, :], -1.0)
    pr.step()
    pr.op("act", "activation", h["lam"][:, :], h["w"][:, :], AF.Exp)
    pwr = ctx.pb[0]
    for k in range(KT):
        t = pr.op("pe", "transpose", pwr[0:1, k * 128:(k + 1) * 128] if False else pwr[0:1, 0:128],
                  h["w"][:, k:k + 1], ctx.eye[:, :])
        pr.op("dve", "tensor_copy", h["wrow"][:, k * 128:(k + 1) * 128], pwr[0:1, 0:128], deps=[t])
        pr.step()
    ps0 = ctx.pb[0]
    pr.op("dve", "tensor_mul", h["sqhal"][:, :], h["xhal"][:, :], h["xhal"][:, :])
    pr.step()
    pr.op("pe", "matmul", ps0[0:1, 0:KT], ctx.inv_col, h["xhal"][:, :], start=True, stop=True)
    pr.op("pe", "matmul", ps0[0:1, 8:8 + KT], ctx.inv_col, h["sqhal"][:, :], start=True, stop=True)
    pr.step()
    rm = pr.op("dve", "reduce_sum", h["hsc"][:, 0:1], ps0[0:1, 0:KT], axis=AX)
    rv = pr.op("dve", "reduce_sum", h["hsc"][:, 1:2], ps0[0:1, 8:8 + KT], axis=AX)
    pr.step()
    pr.op("pool", "tensor_mul", h["hsc"][:, 2:3], h["hsc"][:, 0:1], h["hsc"][:, 0:1], deps=[rm])
    pr.step()
    pr.op("pool", "tensor_sub", h["hsc"][:, 2:3], h["hsc"][:, 1:2], h["hsc"][:, 2:3], deps=[rv])
    pr.step()
    pr.op("act", "activation", h["hsc"][:, 3:4], h["hsc"][:, 2:3], AF.Ln, bias=ctx.eps11)
    pr.op("act", "activation", h["hsc"][:, 3:4], h["hsc"][:, 3:4], AF.Exp, scale=-0.5)
    pr.step()
    pr.op("pool", "tensor_mul", h["hsc"][:, 0:1], h["hsc"][:, 0:1], h["hsc"][:, 3:4])
    pr.step()
    pr.op("pe", "matmul", ps0[:, 0:1], ctx.onesrow[:, :], h["hsc"][:, 0:1], start=True, stop=True)
    pr.op("pe", "matmul", ps0[:, 1:2], ctx.onesrow[:, :], h["hsc"][:, 3:4], start=True, stop=True)
    pr.step()
    pr.op("dve", "tensor_copy", h["hrep"][:, :], ps0[:, 0:2])
    pr.step()
    pr.op("dve", "scalar_tensor_tensor", h["xnh"][:, :], h["xhal"][:, :], h["hrep"][:, 1:2],
          h["hrep"][:, 0:1].broadcast_to((128, KT)), OP.mult, OP.subtract)
    pr.step()
    pr.op("dve", "tensor_scalar_mul", h["xnh"][:, :], h["xnh"][:, :], h["hmask"][:, 0:1])
    pr.step()
    return h

def _emit_l1a_half(ctx, nm, h, half, NT, do_mix_qty):
    pr = ctx.pr
    off = half * NT
    NTH = NT + 1
    for k in range(KT):
        pr.dma("sp", h["x"][:, k * NT:(k + 1) * NT],
               nm["x_d"][k * 128:(k + 1) * 128, off:off + NT])
    if half == 0:
        for k in range(KT):
            pr.op("dve", "tensor_copy", h["xn"][:, k * NTH:k * NTH + 1], h["xnh"][:, k:k + 1])
    else:
        for k in range(KT):
            pr.op("dve", "tensor_copy", h["xn"][:, k * NTH:k * NTH + 1],
                  h["xn"][:, k * NTH + NT:k * NTH + NT + 1])
    pr.step()
    emit_ln_strided(ctx, h["x"], NT, h["xn"], NTH, 1, "ln1a")
    for (qi, dst) in do_mix_qty:
        for k in range(KT):
            t1 = pr.op("dve", "tensor_scalar_mul", h["tmix"][:, 0:NT],
                       h["xn"][:, k * NTH + 1:k * NTH + 1 + NT], h["tmx"][:, qi * KT + k:qi * KT + k + 1])
            pr.op("dve", "scalar_tensor_tensor", dst[:, k * NT:(k + 1) * NT],
                  h["xn"][:, k * NTH:k * NTH + NT], h["otmx"][:, qi * KT + k:qi * KT + k + 1],
                  h["tmix"][:, 0:NT], OP.mult, OP.add, deps=[t1])
    pr.step()

L1A_TRUNC = 99

def build_l1a_p1():
    ctx = Ctx()
    pr = ctx.pr
    NT = 1024
    nm = _l1a_common(ctx, pr, NT)
    summ_d = ctx.outp("summ", [128, KT * 3])
    h = _emit_l1a_prelude(ctx, nm, NT)
    wt = [ctx.sb(f"s_wt{i}", 128, KT * 128, BF16) for i in range(2)]
    Et = [ctx.sb(f"s_E{i}", 128, NT) for i in range(2)]
    vt = [ctx.sb(f"s_v{i}", 128, NT) for i in range(2)]
    kb = ctx.sb("s_kb", 128, NT, BF16)
    wline = ctx.sb("s_wline", 128, NT)
    wrev = ctx.sb("s_wrev", 128, NT)
    dump = ctx.sb("s_dump", 128, NT)
    revr = ctx.sb("s_revr", 1, NT)
    acc = ctx.sb("s_acc", 128, KT * 3)
    accH = [ctx.sb(f"s_accH{i}", 128, KT * 3) for i in range(2)]
    lam2 = ctx.sb("s_lam2", 128, KT)
    pr.op("pool", "iota", revr[:, :], [[-1, NT]], base=NT - 1, channel_multiplier=0,
          allow_small_or_imprecise_dtypes=True)
    pr.op("act", "activation", lam2[:, :], h["w"][:, :], AF.Exp, scale=float(NT))
    pr.step()
    for half in range(2):
        _emit_l1a_half(ctx, nm, h, half, NT, [(0, h["xk"]), (1, h["xv"])])
        wdma = [None, None]; wlast = [pr.last["pe"], pr.last["pe"]]
        vdma = [None, None]; vlast = [None, None]
        bankk = {0: None}; bankv = {0: None}
        for m in range(KT if L1A_TRUNC >= 1 else 0):
            i = m % 2
            if wdma[i] is None:
                wdma[i] = pr.dma("pool", wt[i][:, :].rearrange("p (k o) -> p k o", k=KT),
                                 nm["wk_d"][m].rearrange("k c o -> c k o"), nobar=True, deps=[wlast[i]])
            kpsum = ctx.pb[4]
            vpsum = ctx.pb[5]
            mmk = mmv = None
            for blk in range(NT // 512):
                for k in range(KT):
                    mmk = pr.op("pe", "matmul", kpsum[:, :] if NT == 512 else kpsum[:, :],
                               wt[i][:, k * 128:(k + 1) * 128],
                               h["xk"][:, k * NT + blk * 512:k * NT + blk * 512 + 512],
                               start=(k == 0), stop=(k == KT - 1),
                               deps=[wdma[i]] + ([bankk[0]] if k == 0 else []), nobar=(k > 0))
                ek = pr.op("act", "activation", Et[i][:, blk * 512:blk * 512 + 512],
                           kpsum[:, :], AF.Exp, deps=[mmk], nobar=True)
                bankk[0] = pr.op("act", "copy", kb[:, blk * 512:blk * 512 + 512], kpsum[:, :],
                                 deps=[mmk, ek], nobar=True)
            wlast[i] = pr.last["pe"]
            if L1A_TRUNC < 2:
                pr.step()
                continue
            if m + 1 < KT:
                wdma[(m + 1) % 2] = pr.dma("pool", wt[(m + 1) % 2][:, :].rearrange("p (k o) -> p k o", k=KT),
                                           nm["wk_d"][m + 1].rearrange("k c o -> c k o"),
                                           nobar=True, deps=[wlast[(m + 1) % 2]])
            else:
                wdma[(m + 1) % 2] = None
            vtw = ctx.sb_shared("s_vtw", 128, KT * 128, BF16)
            vd = pr.dma("pool", vtw[:, :].rearrange("p (k o) -> p k o", k=KT),
                        nm["wv_d"][m].rearrange("k c o -> c k o"), deps=[vlast[0] if vlast[0] else None], nobar=True)
            for blk in range(NT // 512):
                for k in range(KT):
                    mmv = pr.op("pe", "matmul", vpsum[:, :],
                               vtw[:, k * 128:(k + 1) * 128],
                               h["xv"][:, k * NT + blk * 512:k * NT + blk * 512 + 512],
                               start=(k == 0), stop=(k == KT - 1),
                               deps=[vd] + ([bankv[0]] if k == 0 else []), nobar=(k > 0))
                bankv[0] = pr.op("act", "copy", vt[i][:, blk * 512:blk * 512 + 512], vpsum[:, :],
                                 deps=[mmv], nobar=True)
            vlast[0] = pr.last["pe"]
            pr.step()
            if L1A_TRUNC < 3:
                pr.step()
                continue
            pw = ctx.pb[2]
            for blk in range(NT // 512):
                t = pr.op("pe", "matmul", pw[:, :], h["wrow"][:, m * 128:(m + 1) * 128],
                          revr[:, blk * 512:blk * 512 + 512], start=True, stop=True)
                c1 = pr.op("act", "copy", wline[:, blk * 512:blk * 512 + 512], pw[:, :], deps=[t])
                pr.op("act", "activation", wrev[:, blk * 512:blk * 512 + 512], pw[:, :], AF.Exp, deps=[c1])
                pr.step()
            if L1A_TRUNC < 4:
                pr.step()
                continue
            ev = pr.op("dve", "tensor_mul", dump[:, :], Et[i][:, :], vt[i][:, :])
            pr.step()
            pr.op("dve", "scalar_tensor_tensor", dump[:, :], dump[:, :], 1.0, wrev[:, :],
                  OP.mult, OP.mult, accum_out=acc[:, m * 3 + 0:m * 3 + 1])
            pr.step()
            pr.op("dve", "scalar_tensor_tensor", dump[:, :], Et[i][:, :], 1.0, wrev[:, :],
                  OP.mult, OP.mult, accum_out=acc[:, m * 3 + 1:m * 3 + 2])
            pr.step()
            pr.op("pool", "tensor_add", dump[:, :], kb[:, :], wline[:, :])
            pr.step()
            pr.op("dve", "reduce_max", acc[:, m * 3 + 2:m * 3 + 3], dump[:, :], axis=AX)
            pr.step()
        pr.op("pool", "tensor_copy", accH[half][:, :], acc[:, :])
        pr.step()
    for m in range(KT):
        pr.op("dve", "scalar_tensor_tensor", acc[:, m * 3:m * 3 + 1], accH[0][:, m * 3:m * 3 + 1],
              lam2[:, m:m + 1], accH[1][:, m * 3:m * 3 + 1], OP.mult, OP.add)
        pr.op("dve", "scalar_tensor_tensor", acc[:, m * 3 + 1:m * 3 + 2], accH[0][:, m * 3 + 1:m * 3 + 2],
              lam2[:, m:m + 1], accH[1][:, m * 3 + 1:m * 3 + 2], OP.mult, OP.add)
        pr.op("pool", "tensor_scalar", dump[:, 0:1], h["w"][:, m:m + 1], float(NT), 0.0, OP.mult, op1=OP.add)
        pr.step()
        pr.op("pool", "tensor_add", dump[:, 1:2], accH[0][:, m * 3 + 2:m * 3 + 3], dump[:, 0:1])
        pr.step()
        pr.op("dve", "tensor_max", acc[:, m * 3 + 2:m * 3 + 3], dump[:, 1:2], accH[1][:, m * 3 + 2:m * 3 + 3])
        pr.step()
    pr.dma("sp", summ_d[:, :], acc[:, :])
    pr.step()
    pr.finalize()
    return ctx.nc

def build_l1a_p2():
    ctx = Ctx()
    pr = ctx.pr
    NT = 1024
    NTH = NT + 1
    nm = _l1a_common(ctx, pr, NT)
    wr_d = ctx.inp("wrh", [KT, KT, 128, 128])
    wo_d = ctx.inp("woh", [KT, KT, 128, 128])
    tf_d = ctx.inp("tfc", [128, KT])
    sg_d = ctx.inp("summg", [128, KT * 3 * 8])
    ms_d = ctx.inp("msteps", [1, 8])
    xo_d = ctx.outp("xo", [D, TC])
    h = _emit_l1a_prelude(ctx, nm, NT)
    xr = h["xk"]
    r = ctx.sb("s_r", 128, KT * NT, BF16)
    ry = ctx.sb_shared("ln1a_sq", 128, KT * NT, BF16)
    Eb = ctx.sb("s_Eb", 128, KT * NT, BF16)
    EVb = ctx.sb("s_EVb", 128, KT * NT, BF16)
    vf = [ctx.sb(f"s_vf{i}", 128, 512, BF16) for i in range(2)]
    tf = ctx.sb("s_tf", 128, KT)
    eu = ctx.sb("s_eu", 128, KT)
    sg = ctx.sb("s_sg", 128, KT * 3 * 8)
    ms = ctx.sb("s_ms", 1, 8)
    logF = ctx.sb("s_logF", 128, 8)
    FF = ctx.sb("s_FF", 128, 8)
    d8 = ctx.sb("s_d8", 128, 8)
    accR = ctx.sb("s_accR", 128, KT * 3)
    carry = ctx.sb("s_carry", 128, KT * 3)
    aab = ctx.sb("s_aab", 128, NTH)
    bbb = ctx.sb("s_bbb", 128, NTH)
    ppb = ctx.sb("s_ppb", 128, NTH)
    _xvf = None
    lnE = ctx.sb("s_lnE", 128, NT)
    wt = [ctx.sb(f"s_wt{i}", 128, KT * 128, BF16) for i in range(2)]

    _xvf = h["xv"][:, :].bitcast(F32)
    aaP = _xvf[:, 0:NT]
    bbP = _xvf[:, NT:2 * NT]
    ppP = _xvf[:, 2 * NT:3 * NT]
    num, den, psc = aaP, bbP, ppP
    pr.dma("sp", sg[:, :], sg_d[:, :])
    pr.dma("sp", ms[:, :], ms_d[:, :])
    pr.dma("sp", tf[:, :], tf_d[:, :])
    pr.step()
    pr.op("act", "activation", eu[:, :], tf[:, :], AF.Exp)
    pr.step()
    pF = ctx.pb[1]
    for m in range(KT):
        t = pr.op("pe", "matmul", pF[:, 0:8], h["wrow"][:, m * 128:(m + 1) * 128],
                  ms[:, :], start=True, stop=True)
        pr.op("act", "copy", logF[:, :], pF[:, 0:8], deps=[t])
        pr.op("act", "activation", FF[:, :], pF[:, 0:8], AF.Exp, deps=[t])
        pr.step()
        pr.op("dve", "scalar_tensor_tensor", d8[:, :], sg[:, (m * 3 + 0) * 8:(m * 3 + 0) * 8 + 8],
              1.0, FF[:, :], OP.mult, OP.mult, accum_out=accR[:, m * 3:m * 3 + 1])
        pr.step()
        pr.op("dve", "scalar_tensor_tensor", d8[:, :], sg[:, (m * 3 + 1) * 8:(m * 3 + 1) * 8 + 8],
              1.0, FF[:, :], OP.mult, OP.mult, accum_out=accR[:, m * 3 + 1:m * 3 + 2])
        pr.step()
        pr.op("pool", "tensor_add", d8[:, :], sg[:, (m * 3 + 2) * 8:(m * 3 + 2) * 8 + 8], logF[:, :])
        pr.step()
        pr.op("dve", "reduce_max", accR[:, m * 3 + 2:m * 3 + 3], d8[:, :], axis=AX)
        pr.step()
    pr.op("pool", "tensor_copy", carry[:, :], accR[:, :])
    pr.step()

    for half in range(2):
        off = half * NT
        _emit_l1a_half(ctx, nm, h, half, NT, [(0, h["xk"]), (1, h["xv"])])
        for (w_src, dst, mode) in ((nm["wk_d"], None, "k"), (nm["wv_d"], None, "v"), (wr_d, r, "r")):
            pass
        wdma = [None, None]; wlast = [pr.last["pe"], pr.last["pe"]]
        bankk = {0: None, 1: None}; bankv = {0: None, 1: None}
        vtw = ctx.sb_shared("s_vtw", 128, KT * 128, BF16)
        vlast = [pr.last["pe"]]
        for m in range(KT):
            i = m % 2
            if wdma[i] is None:
                wdma[i] = pr.dma("pool", wt[i][:, :].rearrange("p (k o) -> p k o", k=KT),
                                 nm["wk_d"][m].rearrange("k c o -> c k o"), nobar=True, deps=[wlast[i]])
            vd = pr.dma("pool", vtw[:, :].rearrange("p (k o) -> p k o", k=KT),
                        nm["wv_d"][m].rearrange("k c o -> c k o"), nobar=True, deps=[vlast[0]])
            for blk in range(NT // 512):
                kps = ctx.pb[4]; vps = ctx.pb[5]
                mmk = mmv = None
                for k in range(KT):
                    mmk = pr.op("pe", "matmul", kps[:, :], wt[i][:, k * 128:(k + 1) * 128],
                               h["xk"][:, k * NT + blk * 512:k * NT + blk * 512 + 512],
                               start=(k == 0), stop=(k == KT - 1),
                               deps=[wdma[i]] + ([bankk[blk % 2]] if k == 0 else []), nobar=(k > 0))
                sl = slice(m * NT + blk * 512, m * NT + blk * 512 + 512)
                e2 = pr.op("act", "activation", Eb[:, sl], kps[:, :], AF.Exp, deps=[mmk], nobar=True)
                bankk[blk % 2] = e2
                for k in range(KT):
                    mmv = pr.op("pe", "matmul", vps[:, :], vtw[:, k * 128:(k + 1) * 128],
                               h["xv"][:, k * NT + blk * 512:k * NT + blk * 512 + 512],
                               start=(k == 0), stop=(k == KT - 1),
                               deps=[vd] + ([bankv[blk % 2]] if k == 0 else []), nobar=(k > 0))
                vv = pr.op("act", "copy", vf[blk % 2][:, :], vps[:, :], deps=[mmv], nobar=True)
                bankv[blk % 2] = vv
                pr.op("dve", "tensor_mul", EVb[:, sl], Eb[:, sl], vf[blk % 2][:, :],
                      deps=[e2, vv], nobar=True)
            wlast[i] = pr.last["pe"]
            vlast[0] = pr.last["pe"]
            if m + 1 < KT:
                wdma[(m + 1) % 2] = pr.dma("pool", wt[(m + 1) % 2][:, :].rearrange("p (k o) -> p k o", k=KT),
                                           nm["wk_d"][m + 1].rearrange("k c o -> c k o"),
                                           nobar=True, deps=[wlast[(m + 1) % 2]])
            else:
                wdma[(m + 1) % 2] = None
        pr.step()
        for k in range(KT):
            t1 = pr.op("dve", "tensor_scalar_mul", h["tmix"][:, 0:NT],
                       h["xn"][:, k * NTH + 1:k * NTH + 1 + NT], h["tmx"][:, 2 * KT + k:2 * KT + k + 1])
            pr.op("dve", "scalar_tensor_tensor", xr[:, k * NT:(k + 1) * NT],
                  h["xn"][:, k * NTH:k * NTH + NT], h["otmx"][:, 2 * KT + k:2 * KT + k + 1],
                  h["tmix"][:, 0:NT], OP.mult, OP.add, deps=[t1])
        pr.step()
        emit_sq_proj(ctx, wr_d, xr, r, NT, act="sigmoid")
        for m in range(KT):
            lamt = h["lam"][:, m:m + 1].broadcast_to((128, NT))
            wt_t = h["w"][:, m:m + 1].broadcast_to((128, NT))
            pr.op("act", "activation", lnE[:, :], Eb[:, m * NT:(m + 1) * NT], AF.Ln)
            pr.op("pool", "tensor_copy", aab[:, 0:1], carry[:, m * 3:m * 3 + 1])
            pr.op("pool", "tensor_copy", bbb[:, 0:1], carry[:, m * 3 + 1:m * 3 + 2])
            pr.op("pool", "tensor_copy", ppb[:, 0:1], carry[:, m * 3 + 2:m * 3 + 3])
            pr.step()
            pr.op("dve", "tensor_tensor_scan", aab[:, 1:NTH], lamt, EVb[:, m * NT:(m + 1) * NT],
                  carry[:, m * 3:m * 3 + 1], OP.mult, OP.add)
            pr.op("dve", "tensor_tensor_scan", bbb[:, 1:NTH], lamt, Eb[:, m * NT:(m + 1) * NT],
                  carry[:, m * 3 + 1:m * 3 + 2], OP.mult, OP.add)
            pr.op("dve", "tensor_tensor_scan", ppb[:, 1:NTH], wt_t, lnE[:, :],
                  carry[:, m * 3 + 2:m * 3 + 3], OP.add, OP.max)
            pr.step()
            pr.op("pool", "tensor_copy", carry[:, m * 3:m * 3 + 1], aab[:, NT:NTH])
            pr.op("pool", "tensor_copy", carry[:, m * 3 + 1:m * 3 + 2], bbb[:, NT:NTH])
            pr.op("pool", "tensor_copy", carry[:, m * 3 + 2:m * 3 + 3], ppb[:, NT:NTH])
            sl = slice(m * NT, (m + 1) * NT)
            pr.op("pool", "tensor_copy", aaP, aab[:, 0:NT])
            pr.op("pool", "tensor_copy", bbP, bbb[:, 0:NT])
            pr.op("pool", "tensor_copy", ppP, ppb[:, 0:NT])
            pr.step()
            pr.op("dve", "scalar_tensor_tensor", psc, ppP, tf[:, m:m + 1],
                  lnE[:, :], OP.add, OP.max)
            pr.op("dve", "scalar_tensor_tensor", num, aaP, eu[:, m:m + 1],
                  EVb[:, sl], OP.mult, OP.add)
            pr.op("dve", "scalar_tensor_tensor", den, bbP, eu[:, m:m + 1],
                  Eb[:, sl], OP.mult, OP.add)
            pr.step()
            pr.op("act", "activation", psc, psc, AF.Exp, bias=ctx.ln6_col)
            pr.step()
            pr.op("dve", "tensor_add", den, den, psc)
            pr.step()
            pr.op("act", "activation", den, den, AF.Ln)
            pr.step()
            pr.op("act", "activation", den, den, AF.Exp, scale=-1.0)
            pr.step()
            pr.op("dve", "tensor_mul", num, num, den)
            pr.step()
            pr.op("dve", "tensor_mul", ry[:, sl], num, r[:, sl])
            pr.step()
        wdma = [None, None]; wlast = [pr.last["pe"], pr.last["pe"]]
        bankr = {0: None, 1: None}
        for c in range(KT):
            i = c % 2
            if wdma[i] is None:
                wdma[i] = pr.dma("pool", wt[i][:, :].rearrange("p (k o) -> p k o", k=KT),
                                 wo_d[c].rearrange("k c o -> c k o"), nobar=True, deps=[wlast[i]])
            for blk in range(NT // 512):
                bank = ctx.pb[4 + (blk % 2)]
                mm = None
                for k in range(KT):
                    mm = pr.op("pe", "matmul", bank[:, :], wt[i][:, k * 128:(k + 1) * 128],
                               ry[:, k * NT + blk * 512:k * NT + blk * 512 + 512],
                               start=(k == 0), stop=(k == KT - 1),
                               deps=[wdma[i]] + ([bankr[blk % 2]] if k == 0 else []), nobar=(k > 0))
                sl = slice(c * NT + blk * 512, c * NT + blk * 512 + 512)
                t1 = pr.op("dve", "tensor_add", h["x"][:, sl], h["x"][:, sl], bank[:, :],
                           deps=[mm], nobar=True)
                bankr[blk % 2] = t1
                pr.dma("sp", xo_d[c * 128:(c + 1) * 128, off + blk * 512:off + blk * 512 + 512],
                       h["x"][:, sl], deps=[t1], nobar=True)
            wlast[i] = pr.last["pe"]
            if c + 1 < KT:
                wdma[(c + 1) % 2] = pr.dma("pool", wt[(c + 1) % 2][:, :].rearrange("p (k o) -> p k o", k=KT),
                                           wo_d[c + 1].rearrange("k c o -> c k o"),
                                           nobar=True, deps=[wlast[(c + 1) % 2]])
            else:
                wdma[(c + 1) % 2] = None
        pr.step()
    pr.finalize()
    return ctx.nc

def prep_l1a(inp):
    def wh(W):
        return np.ascontiguousarray(W.reshape(KT, 128, KT, 128).transpose(0, 2, 3, 1))
    td = inp["L1_attn_time_decay"]; tfirst = inp["L1_attn_time_first"]
    tmx = np.concatenate([inp["L1_attn_time_mix_k"].reshape(-1),
                          inp["L1_attn_time_mix_v"].reshape(-1),
                          inp["L1_attn_time_mix_r"].reshape(-1)])
    return {"wkh": wh(inp["L1_attn_Wk"]), "wvh": wh(inp["L1_attn_Wv"]),
            "wrh": wh(inp["L1_attn_Wr"]), "woh": wh(inp["L1_attn_Wo"]),
            "tdc": np.ascontiguousarray(td.reshape(KT, 128).T),
            "tfc": np.ascontiguousarray(tfirst.reshape(KT, 128).T),
            "tmx": np.ascontiguousarray(tmx.reshape(3 * KT, 128).T)}

def _l0a_common(ctx):
    nm = {}
    nm["x_d"] = ctx.inp("x", [D, TC])
    nm["wq_d"] = ctx.inp("wqh", [KT, KT, 128, 128])
    nm["wkc_d"] = ctx.inp("wkch", [KT, KT, 128, 128])
    nm["wkx_d"] = ctx.inp("wkxh", [KT, 128, D])
    nm["wvx_d"] = ctx.inp("wvxh", [KT, 128, D])
    nm["triu_d"] = ctx.inp("triuI", [128, 128])
    return nm

def _emit_elu1_evac(ctx, psum, dst, sc1, sc2, deps):
    pr = ctx.pr
    prev = getattr(ctx, "_elu_prev", None)
    a1 = pr.op("act", "activation", sc1[:, :], psum, AF.Relu, scale=-1.0,
               deps=list(deps) + [prev])
    a2 = pr.op("act", "activation", sc1[:, :], sc1[:, :], AF.Exp, scale=-1.0, deps=[a1])
    a3 = pr.op("act", "activation", sc2[:, :], psum, AF.Relu, deps=[a2])
    out = pr.op("dve", "tensor_add", dst, sc1[:, :], sc2[:, :], deps=[a2, a3])
    ctx._elu_prev = out
    return out

def _emit_l0a_proj_tm(ctx, nm, xn, NT, ktm, vtm, elu_sc):
    pr = ctx.pr
    nch = NT // 128
    for c in range(nch):
        for half_ch in range(2):
            kps = ctx.pb[4]
            vps = ctx.pb[5]
            mmk = mmv = None
            for kc in range(KT):
                lhs = xn[:, kc * NT + c * 128: kc * NT + (c + 1) * 128]
                mmk = pr.op("pe", "matmul", kps[:, :], lhs,
                            nm["wkx_s"][:, kc * D + half_ch * 512: kc * D + half_ch * 512 + 512],
                            start=(kc == 0), stop=(kc == KT - 1), nobar=(kc > 0),
                            deps=[nm["wkx_tok"]])
            for kc in range(KT):
                lhs = xn[:, kc * NT + c * 128: kc * NT + (c + 1) * 128]
                mmv = pr.op("pe", "matmul", vps[:, :], lhs,
                            nm["wvx_s"][:, kc * D + half_ch * 512: kc * D + half_ch * 512 + 512],
                            start=(kc == 0), stop=(kc == KT - 1), nobar=(kc > 0),
                            deps=[nm["wvx_tok"]])
            sl = slice(c * D + half_ch * 512, c * D + half_ch * 512 + 512)
            _emit_elu1_evac(ctx, kps[:, :], ktm[:, sl], elu_sc[0], elu_sc[1], [mmk])
            pr.op("act", "copy", vtm[:, sl], vps[:, :], deps=[mmv], nobar=True)
            pr.step()

def build_l0a_p1():
    ctx = Ctx()
    pr = ctx.pr
    NT = 1024
    nm = _l0a_common(ctx)
    summ_d = ctx.outp("summ", [128, 520])
    x = ctx.sb("s_x", 128, KT * NT)
    xn = ctx.sb("s_xn", 128, KT * NT, BF16)
    ktm = ctx.sb("s_ktm", 128, (NT // 128) * D, BF16)
    vtm = ctx.sb("s_vtm", 128, (NT // 128) * D, BF16)
    elu_sc = [ctx.sb(f"s_esc{i}", 128, 512) for i in range(2)]
    ssum = ctx.sb("s_ssum", 128, 520)
    onesb = ctx.sb("s_onesb", 128, 1, BF16)
    nm["wkx_s"] = ctx.sb("s_wkx", 128, KT * D, BF16)
    nm["wvx_s"] = ctx.sb("s_wvx", 128, KT * D, BF16)
    nm["wkx_tok"] = pr.dma("pool", nm["wkx_s"][:, :].rearrange("p (k o) -> p k o", k=KT),
                           nm["wkx_d"].rearrange("k p o -> p k o"))
    nm["wvx_tok"] = pr.dma("pool", nm["wvx_s"][:, :].rearrange("p (k o) -> p k o", k=KT),
                           nm["wvx_d"].rearrange("k p o -> p k o"))
    pr.op("dve", "memset", onesb[:, :], 1.0)
    pr.step()
    ctx.rep_banks = (4, 5)
    kvps = ctx.pb[3]
    ksps = ctx.pb[2]
    first = True
    for half in range(2):
        off = half * NT
        for k in range(KT):
            pr.dma("sp", x[:, k * NT:(k + 1) * NT],
                   nm["x_d"][k * 128:(k + 1) * 128, off:off + NT])
        pr.step()
        emit_ln(ctx, x, NT, xn, "ln0a")
        _emit_l0a_proj_tm(ctx, nm, xn, NT, ktm, vtm, elu_sc)
        for c in range(NT // 128):
            for h in range(16):
                pb_ = (h % 2) * 64
                co = (h // 2) * 64
                ksl = slice(c * D + h * 64, c * D + (h + 1) * 64)
                pr.op("pe", "matmul", kvps[pb_:pb_ + 64, co:co + 64],
                      ktm[:, ksl], vtm[:, ksl],
                      start=first, stop=False, nobar=True)
                pr.op("pe", "matmul", ksps[pb_:pb_ + 64, (h // 2):(h // 2) + 1],
                      ktm[:, ksl], onesb[:, :],
                      start=(first and h == 0), stop=False, nobar=True)
                first = False
        pr.step()
    pr.op("dve", "tensor_copy", ssum[:, 0:512], kvps[:, :])
    pr.op("dve", "tensor_copy", ssum[:, 512:520], ksps[:, 0:8])
    pr.step()
    pr.dma("sp", summ_d[:, :], ssum[:, :])
    pr.step()
    pr.finalize()
    return ctx.nc

def prep_l0a(inp):
    w_qkv = inp["L0_attn_w_qkv"]
    wq = w_qkv[:, 0:1024]; wk = w_qkv[:, 1024:2048]; wv = w_qkv[:, 2048:3072]
    wg = inp["L0_attn_w_gate"]
    wo = inp["L0_attn_w_out"]
    def whT(W):
        return np.ascontiguousarray(W.reshape(KT, 128, KT, 128).transpose(2, 0, 1, 3))
    def wx(W):
        return np.ascontiguousarray(W.reshape(KT, 128, D))
    triu = np.triu(np.ones((128, 128), np.float32))
    return {"wqh": whT(wq), "wkch": whT(wk), "wkxh": wx(wk), "wvxh": wx(wv),
            "wgh": whT(wg), "woh": whT(wo), "triuI": triu}

DEBUG_L0A = False

def build_l0a_p2():
    ctx = Ctx()
    pr = ctx.pr
    NT = 1024
    nm = _l0a_common(ctx)
    wg_d = ctx.inp("wgh", [KT, KT, 128, 128])
    wo_d = ctx.inp("woh", [KT, KT, 128, 128])
    sg_d = ctx.inp("summg", [128, 8 * 520])
    mrow_d = ctx.inp("mcol", [128, 8])
    xo_d = ctx.outp("xo", [D, TC])
    if DEBUG_L0A:
        dbg_yg_d = ctx.outp("dbg_yg", [128, KT * NT], BF16)
        dbg_dn_d = ctx.outp("dbg_dn", [1, 2 * NT])
        dbg_kv_d = ctx.outp("dbg_kv", [128, 512], BF16)
        dbg_ktm_d = ctx.outp("dbg_ktm", [128, 8 * D], BF16)
        dbg_vtm_d = ctx.outp("dbg_vtm", [128, 8 * D], BF16)
        dbg_q_d = ctx.outp("dbg_q", [128, KT * NT], BF16)
        dbg_kc_d = ctx.outp("dbg_kc", [128, NT])

    x = ctx.sb("s_x", 128, KT * NT)
    xn = ctx.sb("s_xn", 128, KT * NT, BF16)
    qcm = ctx.sb("s_qcm", 128, KT * NT, BF16)
    kcm = ctx.sb("s_kcm", 128, KT * NT, BF16)
    gcm = ctx.sb("s_gcm", 128, KT * NT, BF16)
    ktm = ctx.sb("s_ktm", 128, (NT // 128) * D, BF16)
    vtm = ctx.sb("s_vtm", 128, (NT // 128) * D, BF16)
    yg = ctx.sb_shared("ln0a_sq", 128, KT * NT, BF16)
    elu_sc = [ctx.sb(f"s_esc{i}", 128, 512) for i in range(2)]
    mcol = ctx.sb("s_mcol", 128, 8)
    kvin = ctx.sb("s_kvin", 128, 512)
    kvf = ctx.sb("s_kvf", 128, 512)
    kvsb = ctx.sb("s_kvsb", 128, 512, BF16)
    ksin = ctx.sb("s_ksin", 128, 8)
    kcum = ctx.sb("s_kcum", 128, NT)
    qk = ctx.sb("s_qk", 128, NT)
    den1 = ctx.sb("s_den1", 1, 2 * NT)
    carryk = ctx.sb("s_carryk", 128, 8)
    ygsc = ctx.sb("s_ygsc", 128, 128)
    nm["wkx_s"] = qcm
    nm["wvx_s"] = gcm
    sg = x[:, 0:8 * 520]
    wt = [ctx.sb_shared(f"sqp_w{i}", 128, KT * 128, BF16) for i in range(2)]
    triub = ctx.sb("s_triub", 128, 128, BF16)
    pr.dma("pool", triub[:, :], nm["triu_d"][:, :])
    pr.dma("sp", sg[:, :], sg_d[:, :])
    pr.dma("sp", mcol[:, :], mrow_d[:, :])
    pr.step()
    ctx.rep_banks = (4, 5)
    z1 = pr.op("dve", "memset", kvin[:, :], 0.0)
    z2 = pr.op("dve", "memset", ksin[:, :], 0.0)
    pr.step()
    for j in range(8):
        pr.op("dve", "scalar_tensor_tensor", kvin[:, :], sg[:, j * 520:j * 520 + 512],
              mcol[:, j:j + 1], kvin[:, :], OP.mult, OP.add)
        pr.op("dve", "scalar_tensor_tensor", ksin[:, :], sg[:, j * 520 + 512:j * 520 + 520],
              mcol[:, j:j + 1], ksin[:, :], OP.mult, OP.add)
        pr.step()
    pr.op("dve", "tensor_copy", kvsb[:, :], kvin[:, :])
    pr.op("dve", "tensor_copy", kvf[:, :], kvin[:, :])
    pr.op("pool", "tensor_copy", carryk[:, :], ksin[:, :])
    pr.step()
    kvps = ctx.pb[3]
    first_kv = True
    for half in range(2):
        off = half * NT
        for k in range(KT):
            pr.dma("sp", x[:, k * NT:(k + 1) * NT],
                   nm["x_d"][k * 128:(k + 1) * 128, off:off + NT])
        pr.step()
        emit_ln(ctx, x, NT, xn, "ln0a")
        nm["wkx_tok"] = pr.dma("pool", nm["wkx_s"][:, :].rearrange("p (k o) -> p k o", k=KT),
                               nm["wkx_d"].rearrange("k p o -> p k o"))
        nm["wvx_tok"] = pr.dma("pool", nm["wvx_s"][:, :].rearrange("p (k o) -> p k o", k=KT),
                               nm["wvx_d"].rearrange("k p o -> p k o"))
        pr.step()
        _emit_l0a_proj_tm(ctx, nm, xn, NT, ktm, vtm, elu_sc)
        for (wd_, dst, act) in ((nm["wq_d"], qcm, "elu1"), (nm["wkc_d"], kcm, "elu1"),
                                (wg_d, gcm, "sigmoid")):
            wdma = [None, None]; wlast = [pr.last["pe"], pr.last["pe"]]
            bankr = {0: None, 1: None}
            for m in range(KT):
                i = m % 2
                if wdma[i] is None:
                    wdma[i] = pr.dma("pool", wt[i][:, :].rearrange("p (k o) -> p k o", k=KT),
                                     wd_[m].rearrange("k c o -> c k o"), nobar=True, deps=[wlast[i]])
                for blk in range(NT // 512):
                    bank = ctx.pb[0 + (blk % 2)]
                    mm = None
                    for k in range(KT):
                        mm = pr.op("pe", "matmul", bank[:, :], wt[i][:, k * 128:(k + 1) * 128],
                                   xn[:, k * NT + blk * 512:k * NT + blk * 512 + 512],
                                   start=(k == 0), stop=(k == KT - 1),
                                   deps=[wdma[i]] + ([bankr[blk % 2]] if k == 0 else []), nobar=(k > 0))
                    sl = slice(m * NT + blk * 512, m * NT + blk * 512 + 512)
                    if act == "elu1":
                        bankr[blk % 2] = _emit_elu1_evac(ctx, bank[:, :], dst[:, sl],
                                                         elu_sc[0], elu_sc[1], [mm])
                    else:
                        bankr[blk % 2] = pr.op("act", "activation", dst[:, sl], bank[:, :],
                                               AF.Sigmoid, deps=[mm], nobar=True)
                wlast[i] = pr.last["pe"]
                if m + 1 < KT:
                    wdma[(m + 1) % 2] = pr.dma("pool",
                        wt[(m + 1) % 2][:, :].rearrange("p (k o) -> p k o", k=KT),
                        wd_[m + 1].rearrange("k c o -> c k o"), nobar=True,
                        deps=[wlast[(m + 1) % 2]])
                else:
                    wdma[(m + 1) % 2] = None
            pr.step()
        for kk in range(KT):
            sc = pr.op("dve", "tensor_tensor_scan", kcum[:, :],
                       ctx.one_col.broadcast_to((128, NT)), kcm[:, kk * NT:(kk + 1) * NT],
                       carryk[:, kk:kk + 1], OP.mult, OP.add)
            pr.step()
            pr.op("pool", "tensor_copy", carryk[:, kk:kk + 1], kcum[:, NT - 1:NT])
            pr.op("pool", "tensor_copy", qk[:, :], kcum[:, :])
            pr.step()
            pr.op("dve", "tensor_mul", qk[:, :], qk[:, :], qcm[:, kk * NT:(kk + 1) * NT])
            pr.step()
            dps = ctx.pb[0]
            prev_d = None
            for hh in range(2):
                h = 2 * kk + hh
                for blk in range(NT // 512):
                    t = pr.op("pe", "matmul", dps[0:1, 0:512],
                              ctx.cst[hh * 64:hh * 64 + 64, 0:1] if False else ctx.one_col[hh * 64:hh * 64 + 64, :],
                              qk[hh * 64:hh * 64 + 64, blk * 512:blk * 512 + 512],
                              start=True, stop=True, deps=[prev_d] if prev_d else [])
                    prev_d = pr.op("act", "copy", den1[0:1, hh * NT + blk * 512:hh * NT + blk * 512 + 512],
                                   dps[0:1, 0:512], deps=[t])
            pr.step()
            pr.op("act", "activation", den1[:, :], den1[:, :], AF.Ln, bias=ctx.cst[0:1, 3:4])
            pr.step()
            pr.op("act", "activation", den1[:, :], den1[:, :], AF.Exp, scale=-1.0)
            pr.step()
            prev_t7 = None
            prev_t6 = None
            prev_upd = None
            for c in range(NT // 128):
                for hh in range(2):
                    h = 2 * kk + hh
                    pb_ = hh * 64
                    co = (h // 2) * 64
                    cm_sl = slice(kk * NT + c * 128, kk * NT + (c + 1) * 128)
                    tm_sl = slice(c * D + h * 64, c * D + (h + 1) * 64)
                    pps = ctx.pb[1]
                    ops = ctx.pb[2]
                    t1 = pr.op("pe", "matmul", pps[:, 0:128],
                               kcm[pb_:pb_ + 64, cm_sl], qcm[pb_:pb_ + 64, cm_sl],
                               start=True, stop=True, deps=[prev_t7])
                    t2 = pr.op("dve", "tensor_mul", ygsc[:, :].bitcast(BF16)[:, 0:128], pps[:, 0:128],
                               triub[:, :], deps=[t1])
                    t3 = pr.op("pe", "matmul", ops[0:64, 0:128], vtm[:, tm_sl],
                               ygsc[:, :].bitcast(BF16)[:, 0:128], start=True, stop=False,
                               deps=[t2, prev_t6])
                    t4 = pr.op("pe", "matmul", ops[0:64, 0:128], kvsb[pb_:pb_ + 64, co:co + 64],
                               qcm[pb_:pb_ + 64, cm_sl], start=False, stop=True,
                               deps=[t3, prev_upd])
                    t5 = pr.op("pe", "matmul", pps[0:64, 128:256], ctx.onesrow[:, 0:64],
                               den1[0:1, hh * NT + c * 128:hh * NT + (c + 1) * 128],
                               start=True, stop=True, deps=[t2])
                    t6 = pr.op("dve", "tensor_mul", ygsc[0:64, 0:128], ops[0:64, 0:128],
                               gcm[pb_:pb_ + 64, cm_sl], deps=[t4])
                    t7 = pr.op("dve", "tensor_mul", yg[pb_:pb_ + 64, cm_sl], ygsc[0:64, 0:128],
                               pps[0:64, 128:256], deps=[t6, t5])
                    prev_t6 = t6
                    prev_t7 = t7
                pr.step()
                for hh in range(2):
                    h = 2 * kk + hh
                    tm_sl = slice(c * D + h * 64, c * D + (h + 1) * 64)
                    pr.op("pe", "matmul", kvps[hh * 64:hh * 64 + 64, co:co + 64],
                          ktm[:, tm_sl], vtm[:, tm_sl], start=True, stop=True)
                pr.step()
                a1 = pr.op("dve", "tensor_add", kvf[:, co:co + 64],
                           kvf[:, co:co + 64], kvps[:, co:co + 64])
                pr.step()
                prev_upd = pr.op("dve", "tensor_copy", kvsb[:, co:co + 64],
                                 kvf[:, co:co + 64], deps=[a1])
                pr.step()
        if DEBUG_L0A and half == 0:
            pr.dma("sp", dbg_ktm_d[:, :], ktm[:, :])
            pr.dma("sp", dbg_vtm_d[:, :], vtm[:, :])
            pr.dma("sp", dbg_kv_d[:, :], kvsb[:, :])
            pr.dma("sp", dbg_yg_d[:, :], yg[:, :])
            pr.dma("sp", dbg_q_d[:, :], qcm[:, :])
            pr.dma("sp", dbg_dn_d[:, :], den1[:, :])
            pr.dma("sp", dbg_kc_d[:, :], kcum[:, :])
            pr.step()
        wdma = [None, None]; wlast = [pr.last["pe"], pr.last["pe"]]
        bankr = {0: None, 1: None}
        for cc in range(KT):
            i = cc % 2
            if wdma[i] is None:
                wdma[i] = pr.dma("pool", wt[i][:, :].rearrange("p (k o) -> p k o", k=KT),
                                 wo_d[cc].rearrange("k c o -> c k o"), nobar=True, deps=[wlast[i]])
            for blk in range(NT // 512):
                bank = ctx.pb[0 + (blk % 2)]
                mm = None
                for k in range(KT):
                    mm = pr.op("pe", "matmul", bank[:, :], wt[i][:, k * 128:(k + 1) * 128],
                               yg[:, k * NT + blk * 512:k * NT + blk * 512 + 512],
                               start=(k == 0), stop=(k == KT - 1),
                               deps=[wdma[i]] + ([bankr[blk % 2]] if k == 0 else []), nobar=(k > 0))
                sl = slice(cc * NT + blk * 512, cc * NT + blk * 512 + 512)
                t1 = pr.op("dve", "tensor_add", x[:, sl], x[:, sl], bank[:, :], deps=[mm], nobar=True)
                bankr[blk % 2] = t1
                pr.dma("sp", xo_d[cc * 128:(cc + 1) * 128, off + blk * 512:off + blk * 512 + 512],
                       x[:, sl], deps=[t1], nobar=True)
            wlast[i] = pr.last["pe"]
            if cc + 1 < KT:
                wdma[(cc + 1) % 2] = pr.dma("pool",
                    wt[(cc + 1) % 2][:, :].rearrange("p (k o) -> p k o", k=KT),
                    wo_d[cc + 1].rearrange("k c o -> c k o"), nobar=True,
                    deps=[wlast[(cc + 1) % 2]])
            else:
                wdma[(cc + 1) % 2] = None
        pr.step()
    pr.finalize()
    return ctx.nc

L2A_TRUNC = 99

def build_l2a(p2):
    ctx = Ctx()
    pr = ctx.pr
    NT = 1024
    nch = NT // 128
    x_d = ctx.inp("x", [D, TC])
    wk_d = ctx.inp("wkh2", [KT, KT, 128, 128])
    wvx_d = ctx.inp("wvxh2", [KT, 128, D])
    wba_d = ctx.inp("wbah", [KT, 128, 16])
    bba_d = ctx.inp("bba", [16, 1])
    bfa_d = ctx.inp("bfa", [16, 2])
    cs_d = ctx.inp("cosc", [64, TC])
    sn_d = ctx.inp("sinc", [64, TC])
    sel1_d = ctx.inp("sel1h", [8, 8, 128])
    triu_d = ctx.inp("triuI", [128, 128])
    if p2:
        wq_d = ctx.inp("wqh2", [KT, KT, 128, 128])
        wo_d = ctx.inp("woh2", [KT, KT, 128, 128])
        bo_d = ctx.inp("boc", [128, KT])
        sg_d = ctx.inp("summg", [128, 8 * 1024])
        cbt_d = ctx.inp("cbtgT", [8, 8])
        U_d = ctx.inp("Umat", [8, 8])
        badd_d = ctx.inp("badd", [8, 8])
        xo_d = ctx.outp("xo", [D, TC])
    else:
        summ_d = ctx.outp("summ", [128, 1024])
        cbt_o = ctx.outp("cbt", [8, 1])

    x = ctx.sb("s_x", 128, KT * NT)
    xn = ctx.sb("s_xn", 128, KT * NT, BF16)
    kcm = ctx.sb("s_kcm", 128, KT * NT, BF16)
    vtm = ctx.sb("s_vtm", 128, nch * D, BF16)
    ktm = ctx.sb("s_ktm", 128, nch * D, BF16)
    cb = ctx.sb("s_cb", 8, NT + 1)
    lb = ctx.sb("s_lb", 16, NT)
    alp = ctx.sb("s_alp", 8, NT)
    lac = ctx.sb("s_lac", 8, NT)
    cs = ctx.sb("s_cs", 64, NT, BF16)
    sn = ctx.sb("s_sn", 64, NT, BF16)
    sel1 = ctx.sb("s_sel1", 8, 8 * 128)
    triub = ctx.sb("s_triu2", 128, 128)
    rsc = [ctx.sb(f"s_rsc{i}", 64, 512) for i in range(2)]
    bsc = ctx.sb("s_bsc", 128, 512)
    Sf = ctx.sb("s_Sf", 128, 8 * 128)
    Ssb = ctx.sb("s_Ssb", 128, 8 * 128, BF16)
    tcol = ctx.sb("s_tcol", 128, 16)
    pecol = ctx.sb("s_pecol", 128, 8)
    Psc = ctx.sb("s_Psc", 128, 128)
    Pb = ctx.sb("s_Pb", 128, 128, BF16)
    kpt = ctx.sb("s_kpt", 128, 128, BF16)
    ecb = ctx.sb("s_ecb", 128, 128)
    wba = ctx.sb("s_wba", 128, KT * 16, BF16)
    bba = ctx.sb("s_bba", 16, 1)
    bfa = ctx.sb("s_bfa", 16, 2)
    wt = [ctx.sb_shared(f"sqp_w{i}", 128, KT * 128, BF16) for i in range(2)]
    wvx = ctx.sb_shared("ln2a_sq", 128, KT * NT, BF16)
    if p2:
        qcm = ctx.sb("s_qcm", 128, KT * NT, BF16)
        ycm = wvx
        boc = ctx.sb("s_boc", 128, KT)
        Wt8 = ctx.sb("s_Wt8", 8, 8)
        WtT = ctx.sb("s_WtT", 8, 8)
        w8 = ctx.sb("s_w8", 128, 8)
        cbtT = ctx.sb("s_cbtT", 8, 8, BF16)
        Um = ctx.sb("s_Um", 8, 8, BF16)
        badd = ctx.sb("s_badd", 8, 8)
        sg = x[:, 0:8 * 1024]


    pr.dma("sp", sel1[:, :].rearrange("p (e c) -> p e c", e=8), sel1_d.rearrange("e k c -> k e c"))
    pr.dma("sp", triub[:, :], triu_d[:, :])
    pr.dma("pool", wba[:, :].rearrange("p (k e) -> p k e", k=KT), wba_d.rearrange("k p e -> p k e"))
    pr.dma("sp", bba[:, :], bba_d[:, :])
    pr.dma("sp", bfa[:, :], bfa_d[:, :])
    if p2:
        pr.dma("sp", sg[:, :], sg_d[:, :])
        pr.dma("pool", cbtT[:, :], cbt_d[:, :])
        pr.dma("pool", Um[:, :], U_d[:, :])
        pr.dma("sp", badd[:, :], badd_d[:, :])
        pr.dma("sp", boc[:, :], bo_d[:, :])
    pr.step()
    ctx.rep_banks = (4, 5)
    if p2:
        pw = ctx.pb[0]
        t1 = pr.op("pe", "matmul", pw[0:8, 0:8], cbtT[:, :], Um[:, :], start=True, stop=True)
        t2 = pr.op("dve", "tensor_add", Wt8[:, :], pw[0:8, 0:8], badd[:, :], deps=[t1])
        pr.step()
        e1 = pr.op("act", "activation", Wt8[:, :], Wt8[:, :], AF.Exp)
        pr.step()
        tt = pr.op("pe", "transpose", pw[0:8, 8:16], Wt8[:, :], ctx.eye[0:8, 0:8], deps=[e1])
        c1 = pr.op("act", "copy", WtT[:, :], pw[0:8, 8:16], deps=[tt])
        pr.step()
        pr.op("dve", "memset", Sf[:, :], 0.0)
        pr.step()
        for j in range(8):
            tj = pr.op("pe", "matmul", pw[:, 16:24], sel1[:, j * 128:(j + 1) * 128],
                       WtT[:, :].bitcast(BF16)[:, 0:8] if False else WtT[:, :],
                       start=True, stop=True, deps=[c1])
            cj = pr.op("act", "copy", w8[:, :], pw[:, 16:24], deps=[tj])
            pr.step()
            for h in range(8):
                pr.op("dve", "scalar_tensor_tensor", Sf[:, h * 128:(h + 1) * 128],
                      sg[:, j * 1024 + h * 128:j * 1024 + (h + 1) * 128], w8[:, h:h + 1],
                      Sf[:, h * 128:(h + 1) * 128], OP.mult, OP.add)
            pr.step()
    else:
        pr.op("dve", "memset", Sf[:, :], 0.0)
        pr.step()
    pr.op("dve", "tensor_copy", Ssb[:, :], Sf[:, :])
    pr.op("dve", "memset", cb[:, 0:1], 0.0)
    pr.step()

    for half in range(2):
        off = half * NT
        pr.dma("pool", cs[:, :], cs_d[:, off:off + NT])
        pr.dma("pool", sn[:, :], sn_d[:, off:off + NT])
        for k in range(KT):
            pr.dma("sp", x[:, k * NT:(k + 1) * NT],
                   x_d[k * 128:(k + 1) * 128, off:off + NT])
        pr.step()
        emit_ln(ctx, x, NT, xn, "ln2a")
        pba = ctx.pb[0]
        prevc = None
        for blk in range(NT // 512):
            mm = None
            for k in range(KT):
                mm = pr.op("pe", "matmul", pba[0:16, 0:512], wba[:, k * 16:(k + 1) * 16],
                           xn[:, k * NT + blk * 512:k * NT + blk * 512 + 512],
                           start=(k == 0), stop=(k == KT - 1),
                           deps=([prevc] if k == 0 else []))
            prevc = pr.op("act", "activation", lb[:, blk * 512:blk * 512 + 512],
                          pba[0:16, 0:512], AF.Sigmoid, bias=bba[:, 0:1], deps=[mm])
        pr.step()
        pr.op("dve", "tensor_scalar", lb[:, :], lb[:, :], bfa[:, 0:1], bfa[:, 1:2],
              OP.mult, op1=OP.add)
        pr.step()
        pr.dma("sp", alp[:, :], lb[8:16, :])
        pr.step()
        pr.op("act", "activation", lb[0:8, :], lb[0:8, :], AF.Ln)
        pr.step()
        pr.op("dve", "tensor_tensor_scan", cb[:, 1:NT + 1],
              ctx.one_col[0:8, 0:1].broadcast_to((8, NT)), lb[0:8, :],
              cb[:, 0:1], OP.mult, OP.add)
        pr.op("act", "activation", lac[:, :], alp[:, :], AF.Ln)
        pr.step()
        pr.op("dve", "tensor_sub", lac[:, :], lac[:, :], cb[:, 1:NT + 1])
        pr.step()
        if L2A_TRUNC < 1:
            continue
        plist = [(wk_d, kcm)]
        if p2:
            plist.append((wq_d, qcm))
        for (wd_, dst) in plist:
            wdma = [None, None]; wlast = [pr.last["pe"], pr.last["pe"]]
            bankr = {0: None, 1: None}
            for m in range(KT):
                i = m % 2
                if wdma[i] is None:
                    wdma[i] = pr.dma("pool", wt[i][:, :].rearrange("p (k o) -> p k o", k=KT),
                                     wd_[m].rearrange("k c o -> c k o"), nobar=True, deps=[wlast[i]])
                for blk in range(NT // 512):
                    bank = ctx.pb[0 + (blk % 2)]
                    mm = None
                    for k in range(KT):
                        mm = pr.op("pe", "matmul", bank[:, :], wt[i][:, k * 128:(k + 1) * 128],
                                   xn[:, k * NT + blk * 512:k * NT + blk * 512 + 512],
                                   start=(k == 0), stop=(k == KT - 1),
                                   deps=[wdma[i]] + ([bankr[blk % 2]] if k == 0 else []), nobar=(k > 0))
                    sl = slice(m * NT + blk * 512, m * NT + blk * 512 + 512)
                    csl = slice(blk * 512, blk * 512 + 512)
                    r1 = pr.op("dve", "tensor_mul", rsc[0][:, :], bank[0:64, :], cs[:, csl], deps=[mm], nobar=True)
                    r2 = pr.op("dve", "tensor_mul", rsc[1][:, :], bank[64:128, :], sn[:, csl], deps=[mm], nobar=True)
                    o1 = pr.op("dve", "tensor_sub", dst[0:64, sl], rsc[0][:, :], rsc[1][:, :], deps=[r1, r2], nobar=True)
                    r3 = pr.op("dve", "tensor_mul", rsc[0][:, :], bank[64:128, :], cs[:, csl], deps=[o1], nobar=True)
                    r4 = pr.op("dve", "tensor_mul", rsc[1][:, :], bank[0:64, :], sn[:, csl], deps=[o1], nobar=True)
                    o2 = pr.op("dve", "tensor_add", dst[64:128, sl], rsc[0][:, :], rsc[1][:, :], deps=[r3, r4], nobar=True)
                    bankr[blk % 2] = o2
                wlast[i] = pr.last["pe"]
                if m + 1 < KT:
                    wdma[(m + 1) % 2] = pr.dma("pool",
                        wt[(m + 1) % 2][:, :].rearrange("p (k o) -> p k o", k=KT),
                        wd_[m + 1].rearrange("k c o -> c k o"), nobar=True,
                        deps=[wlast[(m + 1) % 2]])
                else:
                    wdma[(m + 1) % 2] = None
            pr.step()
        if L2A_TRUNC < 2:
            continue
        wvd = pr.dma("pool", wvx[:, :].rearrange("p (k o) -> p k o", k=KT),
                     wvx_d.rearrange("k p o -> p k o"), deps=[pr.last["pe"]])
        pr.step()
        prev_ev = None
        for c in range(nch):
            for hb in range(2):
                vps = ctx.pb[4]
                mmv = None
                for kc in range(KT):
                    lhs = xn[:, kc * NT + c * 128:kc * NT + (c + 1) * 128]
                    mmv = pr.op("pe", "matmul", vps[:, :], lhs,
                                wvx[:, kc * D + hb * 512:kc * D + hb * 512 + 512],
                                start=(kc == 0), stop=(kc == KT - 1), nobar=(kc > 0),
                                deps=[wvd] + ([prev_ev] if kc == 0 else []))
                prev_ev = pr.op("act", "copy", vtm[:, c * D + hb * 512:c * D + hb * 512 + 512],
                                vps[:, :], deps=[mmv], nobar=True)
            pr.step()
            for h in range(8):
                tps = ctx.pbb[0]
                t = pr.op("pe", "transpose", tps[:, 0:128],
                          kcm[:, h * NT + c * 128:h * NT + (c + 1) * 128], ctx.eyeb[:, :])
                pr.op("dve", "tensor_copy", ktm[:, c * D + h * 128:c * D + (h + 1) * 128],
                      tps[:, 0:128], deps=[t])
                pr.step()
        for c in range(nch if L2A_TRUNC >= 3 else 0):
            t0 = c * 128
            tps = ctx.pb[1]
            tt1 = pr.op("pe", "transpose", tps[:, 0:8], cb[:, t0 + 1:t0 + 129], ctx.eye[0:8, 0:8])
            tt2 = pr.op("pe", "transpose", tps[:, 8:16], lac[:, t0:t0 + 128], ctx.eye[0:8, 0:8], deps=[tt1])
            tc1 = pr.op("act", "copy", tcol[:, 0:16], tps[:, 0:16], deps=[tt2])
            pr.step()
            for h in range(8):
                hsl = slice(h * NT + t0, h * NT + t0 + 128)
                tmsl = slice(c * D + h * 128, c * D + (h + 1) * 128)
                pw = ctx.pb[0]
                b1 = pr.op("pe", "matmul", pw[:, 0:1], sel1[:, h * 128:(h + 1) * 128],
                           cb[:, t0:t0 + 1], start=True, stop=True)
                b2 = pr.op("pe", "matmul", pw[:, 1:2], sel1[:, h * 128:(h + 1) * 128],
                           cb[:, t0 + 128:t0 + 129], start=True, stop=True, deps=[b1])
                c2 = pr.op("act", "copy", pecol[:, 0:2], pw[:, 0:2], deps=[b2])
                pr.step()
                k1 = pr.op("act", "activation", pecol[:, 2:3], tcol[:, 8 + h:9 + h], AF.Exp, bias=pecol[:, 0:1])
                k2 = pr.op("act", "activation", pecol[:, 3:4], tcol[:, 8 + h:9 + h], AF.Exp, bias=pecol[:, 1:2], deps=[k1])
                l1 = pr.op("dve", "tensor_sub", pecol[:, 5:6], pecol[:, 1:2], pecol[:, 0:1], deps=[c2])
                pr.step()
                l2 = pr.op("act", "activation", pecol[:, 4:5], pecol[:, 5:6], AF.Exp)
                pr.step()
                if p2:
                    pps = ctx.pb[1]
                    p1_ = pr.op("pe", "matmul", pps[:, 0:128], kcm[:, hsl], qcm[:, hsl],
                                start=True, stop=True)
                    p2_ = pr.op("dve", "tensor_scalar_mul", Psc[:, :], pps[:, 0:128],
                                pecol[:, 2:3], deps=[p1_])
                    p3_ = pr.op("dve", "tensor_mul", Pb[:, :], Psc[:, :], triub[:, :], deps=[p2_])
                    ops = ctx.pb[2]
                    o1_ = pr.op("pe", "matmul", ops[:, 0:128], vtm[:, tmsl], Pb[:, :],
                                start=True, stop=False, deps=[p3_])
                    o2_ = pr.op("pe", "matmul", ops[:, 0:128], Ssb[:, h * 128:(h + 1) * 128],
                                qcm[:, hsl], start=False, stop=True, deps=[o1_])
                    e1_ = pr.op("pe", "matmul", pps[:, 128:256], sel1[:, h * 128:(h + 1) * 128],
                                cb[:, t0 + 1:t0 + 129], start=True, stop=True, deps=[p2_])
                    e2_ = pr.op("dve", "tensor_scalar", ecb[:, :], pps[:, 128:256], pecol[:, 0:1],
                                None, OP.subtract, deps=[e1_])
                    pr.step()
                    e3_ = pr.op("act", "activation", ecb[:, :], ecb[:, :], AF.Exp)
                    y1 = pr.op("dve", "tensor_mul", ycm[:, hsl], ops[:, 0:128], ecb[:, :],
                               deps=[o2_, e3_])
                sps = ctx.pb[3]
                ku = pr.op("dve", "tensor_scalar_mul", kpt[:, :], ktm[:, tmsl], pecol[:, 3:4], deps=[k2])
                su = pr.op("pe", "matmul", sps[:, 0:128], kpt[:, :], vtm[:, tmsl],
                           start=True, stop=True, deps=[ku])
                s1_ = pr.op("dve", "scalar_tensor_tensor", Sf[:, h * 128:(h + 1) * 128],
                            Sf[:, h * 128:(h + 1) * 128], pecol[:, 4:5], sps[:, 0:128],
                            OP.mult, OP.add, deps=[su, l2])
                pr.op("dve", "tensor_copy", Ssb[:, h * 128:(h + 1) * 128],
                      Sf[:, h * 128:(h + 1) * 128], deps=[s1_])
                pr.step()
        pr.op("pool", "tensor_copy", cb[:, 0:1], cb[:, NT:NT + 1])
        pr.step()
        if p2:
            wdma = [None, None]; wlast = [pr.last["pe"], pr.last["pe"]]
            bankr = {0: None, 1: None}
            for cc in range(KT):
                i = cc % 2
                if wdma[i] is None:
                    wdma[i] = pr.dma("pool", wt[i][:, :].rearrange("p (k o) -> p k o", k=KT),
                                     wo_d[cc].rearrange("k c o -> c k o"), nobar=True, deps=[wlast[i]])
                for blk in range(NT // 512):
                    bank = ctx.pb[0 + (blk % 2)]
                    mm = None
                    for k in range(KT):
                        mm = pr.op("pe", "matmul", bank[:, :], wt[i][:, k * 128:(k + 1) * 128],
                                   ycm[:, k * NT + blk * 512:k * NT + blk * 512 + 512],
                                   start=(k == 0), stop=(k == KT - 1),
                                   deps=[wdma[i]] + ([bankr[blk % 2]] if k == 0 else []), nobar=(k > 0))
                    sl = slice(cc * NT + blk * 512, cc * NT + blk * 512 + 512)
                    t1 = pr.op("dve", "tensor_scalar", bsc[:, :], bank[:, :], boc[:, cc:cc + 1],
                               None, OP.add, deps=[mm], nobar=True)
                    bankr[blk % 2] = t1
                    t2 = pr.op("dve", "tensor_add", x[:, sl], x[:, sl], bsc[:, :], deps=[t1], nobar=True)
                    pr.dma("sp", xo_d[cc * 128:(cc + 1) * 128, off + blk * 512:off + blk * 512 + 512],
                           x[:, sl], deps=[t2], nobar=True)
                wlast[i] = pr.last["pe"]
                if cc + 1 < KT:
                    wdma[(cc + 1) % 2] = pr.dma("pool",
                        wt[(cc + 1) % 2][:, :].rearrange("p (k o) -> p k o", k=KT),
                        wo_d[cc + 1].rearrange("k c o -> c k o"), nobar=True,
                        deps=[wlast[(cc + 1) % 2]])
                else:
                    wdma[(cc + 1) % 2] = None
            pr.step()
    if not p2:
        pr.dma("sp", summ_d[:, :], Sf[:, :])
        pr.dma("sp", cbt_o[:, :], cb[:, 0:1])
        pr.step()
    pr.finalize()
    return ctx.nc

def prep_l2a(inp, core):
    H, Dh = 8, 128
    perm = np.concatenate([np.arange(0, Dh, 2), np.arange(1, Dh, 2)])
    Wq = inp["L2_attn_Wq"].reshape(H, Dh, D)[:, perm].reshape(D, D)
    Wk = inp["L2_attn_Wk"].reshape(H, Dh, D)[:, perm].reshape(D, D)
    Wv = inp["L2_attn_Wv"]
    def wh(W):
        return np.ascontiguousarray(W.reshape(KT, 128, KT, 128).transpose(0, 2, 3, 1))
    wba = np.concatenate([inp["L2_attn_Wbeta"].T, inp["L2_attn_Walpha"].T], 1)
    bba = np.concatenate([inp["L2_attn_bbeta"], inp["L2_attn_balpha"]]).reshape(16, 1)
    bfa = np.zeros((16, 2), np.float32)
    bfa[0:8, 0] = 0.999 - 0.85; bfa[0:8, 1] = 0.85
    bfa[8:16, 0] = 1.0; bfa[8:16, 1] = 0.0
    half = Dh // 2
    inv_freq = 1.0 / 10000.0 ** (np.arange(half, dtype=np.float32) / half)
    pos = np.arange(core * TC, (core + 1) * TC, dtype=np.float32)
    fr = pos[None, :] * inv_freq[:, None]
    sel1 = np.zeros((8, 8, 128), np.float32)
    for e in range(8):
        sel1[e, e, :] = 1.0
    U = np.zeros((8, 8), np.float32)
    bad = np.full((8, 8), 0.0, np.float32)
    for j in range(8):
        for k in range(8):
            if j < k < core:
                U[k, j] = 1.0
        if j >= core:
            bad[:, j] = -1e30
    return {"wkh2": wh(Wk), "wqh2": wh(Wq), "woh2": wh(inp["L2_attn_Wo"]),
            "wvxh2": np.ascontiguousarray(Wv.T.reshape(KT, 128, D)),
            "wbah": np.ascontiguousarray(wba.reshape(KT, 128, 16)),
            "bba": bba.astype(np.float32), "bfa": bfa,
            "cosc": np.cos(fr).astype(np.float32), "sinc": np.sin(fr).astype(np.float32),
            "sel1h": sel1, "Umat": U, "badd": bad,
            "boc": np.ascontiguousarray(inp["L2_attn_bo"].reshape(KT, 128).T)}

def build_l3a_p1():
    ctx = Ctx()
    pr = ctx.pr
    NT = 1024
    nch = NT // 128
    x_d = ctx.inp("x", [D, TC])
    wk_d = ctx.inp("wkh3", [KT, KT, 128, 128])
    bk_d = ctx.inp("bkc", [128, KT])
    wvx_d = ctx.inp("wvxh3", [KT, 128, D])
    bv_d = ctx.inp("bvr", [1, D])
    ko_d = ctx.outp("kout", [D, TC], BF16)
    vo_d = ctx.outp("vout", [TC, D], BF16)
    x = ctx.sb("s_x", 128, KT * NT)
    xn = ctx.sb("s_xn", 128, KT * NT, BF16)
    kcm = ctx.sb("s_kcm", 128, KT * NT, BF16)
    vtm = ctx.sb("s_vtm", 128, nch * D, BF16)
    bkc = ctx.sb("s_bkc", 128, KT)
    bvr = ctx.sb("s_bvr", 1, D)
    bvrb = ctx.sb("s_bvrb", 1, D, BF16)
    wvx = ctx.sb("s_wvx3", 128, KT * D, BF16)
    orb = ctx.sb("s_orb", 1, 128, BF16)
    wt = [ctx.sb_shared(f"sqp_w{i}", 128, KT * 128, BF16) for i in range(2)]
    pr.dma("sp", bkc[:, :], bk_d[:, :])
    pr.dma("sp", bvr[:, :], bv_d[:, :])
    pr.step()
    pr.op("dve", "tensor_copy", bvrb[:, :], bvr[:, :])
    pr.op("dve", "memset", orb[:, :], 1.0)
    pr.step()
    for half in range(2):
        off = half * NT
        for k in range(KT):
            pr.dma("sp", x[:, k * NT:(k + 1) * NT],
                   x_d[k * 128:(k + 1) * 128, off:off + NT])
        pr.step()
        emit_ln(ctx, x, NT, xn, "ln3a")
        wdma = [None, None]; wlast = [pr.last["pe"], pr.last["pe"]]
        bankr = {0: None, 1: None}
        for m in range(KT):
            i = m % 2
            if wdma[i] is None:
                wdma[i] = pr.dma("pool", wt[i][:, :].rearrange("p (k o) -> p k o", k=KT),
                                 wk_d[m].rearrange("k c o -> c k o"), nobar=True, deps=[wlast[i]])
            for blk in range(NT // 512):
                bank = ctx.pb[0 + (blk % 2)]
                mm = None
                for k in range(KT):
                    mm = pr.op("pe", "matmul", bank[:, :], wt[i][:, k * 128:(k + 1) * 128],
                               xn[:, k * NT + blk * 512:k * NT + blk * 512 + 512],
                               start=(k == 0), stop=(k == KT - 1),
                               deps=[wdma[i]] + ([bankr[blk % 2]] if k == 0 else []), nobar=(k > 0))
                sl = slice(m * NT + blk * 512, m * NT + blk * 512 + 512)
                bankr[blk % 2] = pr.op("act", "activation", kcm[:, sl], bank[:, :],
                                       AF.Identity, bias=bkc[:, m:m + 1], deps=[mm], nobar=True)
                pr.dma("sp", ko_d[m * 128:(m + 1) * 128, off + blk * 512:off + blk * 512 + 512],
                       kcm[:, sl], deps=[bankr[blk % 2]], nobar=True)
            wlast[i] = pr.last["pe"]
            if m + 1 < KT:
                wdma[(m + 1) % 2] = pr.dma("pool",
                    wt[(m + 1) % 2][:, :].rearrange("p (k o) -> p k o", k=KT),
                    wk_d[m + 1].rearrange("k c o -> c k o"), nobar=True,
                    deps=[wlast[(m + 1) % 2]])
            else:
                wdma[(m + 1) % 2] = None
        pr.step()
        wvd = pr.dma("pool", wvx[:, :].rearrange("p (k o) -> p k o", k=KT),
                     wvx_d.rearrange("k p o -> p k o"), deps=[pr.last["pe"]])
        pr.step()
        prev_ev = None
        for c in range(nch):
            for hb in range(2):
                vps = ctx.pb[4]
                mmv = None
                for kc in range(KT):
                    lhs = xn[:, kc * NT + c * 128:kc * NT + (c + 1) * 128]
                    mmv = pr.op("pe", "matmul", vps[:, :], lhs,
                                wvx[:, kc * D + hb * 512:kc * D + hb * 512 + 512],
                                start=(kc == 0), stop=False, nobar=(kc > 0),
                                deps=[wvd] + ([prev_ev] if kc == 0 else []))
                mmv = pr.op("pe", "matmul", vps[:, :], orb[:, :],
                            bvrb[:, hb * 512:hb * 512 + 512], start=False, stop=True,
                            deps=[mmv], nobar=True)
                sl = slice(c * D + hb * 512, c * D + hb * 512 + 512)
                prev_ev = pr.op("act", "copy", vtm[:, sl], vps[:, :], deps=[mmv], nobar=True)
                pr.dma("sp", vo_d[off + c * 128:off + (c + 1) * 128, hb * 512:hb * 512 + 512],
                       vtm[:, sl], deps=[prev_ev], nobar=True)
            pr.step()
    pr.finalize()
    return ctx.nc

def build_l3a_p2():
    ctx = Ctx()
    pr = ctx.pr
    NT = 1024
    nch = NT // 128
    x_d = ctx.inp("x", [D, TC])
    wq_d = ctx.inp("wqh3", [KT, KT, 128, 128])
    bq_d = ctx.inp("bqc", [128, KT])
    kg_d = ctx.inp("kg", [D, T], BF16)
    vg_d = ctx.inp("vg", [T, D], BF16)
    off_d = ctx.inp("offv", [16, T], BF16)
    tn_d = ctx.inp("trilneg", [128, 128])
    klo_d = ctx.inp("klo", [D, TC], BF16)
    vlo_d = ctx.inp("vlo", [TC, D], BF16)
    xo_d = ctx.outp("xo", [D, TC])
    x = ctx.sb("s_x", 128, KT * NT)
    xn = ctx.sb("s_xn", 128, KT * NT, BF16)
    qcm = ctx.sb("s_qcm", 128, KT * NT, BF16)
    kcm = ctx.sb_shared("ln3a2_sq", 128, KT * NT, BF16)
    vtm = ctx.sb("s_vtm", 128, D, BF16)
    bqc = ctx.sb("s_bqc", 128, KT)
    tneg = ctx.sb("s_tneg", 128, 128)
    kbuf = [ctx.sb("s_kb0", 128, KT * 512, BF16)] * 2
    vbuf = [ctx.sb("s_vb0", 128, 4 * D, BF16)] * 2
    PbT = ctx.sb("s_PbT", 128, 4 * 128, BF16)
    Pb = ctx.sb("s_Pb", 128, 512, BF16)
    Ob = [ctx.sb(f"s_Ob{i}", 128, D) for i in range(2)]
    stats = ctx.sb("s_stats", 128, 16)
    linv = ctx.sb("s_linv", 128, 4)
    orb = ctx.sb("s_orb3", 1, 128, BF16)
    offq = [ctx.sb(f"s_offq{i}", 1, T, BF16) for i in range(2)]
    wt = [ctx.sb_shared(f"sqp_w{i}", 128, KT * 128, BF16) for i in range(2)]
    pr.dma("sp", bqc[:, :], bq_d[:, :])
    pr.dma("sp", tneg[:, :], tn_d[:, :])
    pr.step()
    pr.op("dve", "tensor_scalar_mul", bqc[:, :], bqc[:, :], 0.03125)
    pr.op("dve", "memset", orb[:, :], 1.0)
    pr.step()
    for half in range(2):
        off = half * NT
        for k in range(KT):
            pr.dma("sp", x[:, k * NT:(k + 1) * NT],
                   x_d[k * 128:(k + 1) * 128, off:off + NT])
        pr.step()
        emit_ln(ctx, x, NT, xn, "ln3a2")
        wdma = [None, None]; wlast = [pr.last["pe"], pr.last["pe"]]
        bankr = {0: None, 1: None}
        for m in range(KT):
            i = m % 2
            if wdma[i] is None:
                wdma[i] = pr.dma("pool", wt[i][:, :].rearrange("p (k o) -> p k o", k=KT),
                                 wq_d[m].rearrange("k c o -> c k o"), nobar=True, deps=[wlast[i]])
            for blk in range(NT // 512):
                bank = ctx.pb[0 + (blk % 2)]
                mm = None
                for k in range(KT):
                    mm = pr.op("pe", "matmul", bank[:, :], wt[i][:, k * 128:(k + 1) * 128],
                               xn[:, k * NT + blk * 512:k * NT + blk * 512 + 512],
                               start=(k == 0), stop=(k == KT - 1),
                               deps=[wdma[i]] + ([bankr[blk % 2]] if k == 0 else []), nobar=(k > 0))
                sl = slice(m * NT + blk * 512, m * NT + blk * 512 + 512)
                bankr[blk % 2] = pr.op("act", "activation", qcm[:, sl], bank[:, :],
                                       AF.Identity, bias=bqc[:, m:m + 1], scale=0.03125,
                                       deps=[mm], nobar=True)
            wlast[i] = pr.last["pe"]
            if m + 1 < KT:
                wdma[(m + 1) % 2] = pr.dma("pool",
                    wt[(m + 1) % 2][:, :].rearrange("p (k o) -> p k o", k=KT),
                    wq_d[m + 1].rearrange("k c o -> c k o"), nobar=True,
                    deps=[wlast[(m + 1) % 2]])
            else:
                wdma[(m + 1) % 2] = None
        pr.step()
        B = dict(x=x, xn=xn, qcm=qcm, kcm=kcm, vtm=vtm, tneg=tneg, kbuf=kbuf,
                 vbuf=vbuf, PbT=PbT, Pb=Pb, Ob=Ob, stats=stats, linv=linv,
                 orb=orb, offq=offq, kg_d=kg_d, vg_d=vg_d,
                 klo_d=klo_d, vlo_d=vlo_d, off_d=off_d, xo_d=xo_d)
        _l3a_flash(ctx, pr, B, half, NT)
    pr.finalize()
    return ctx.nc

def _l3a_flash(ctx, pr, B, half, NT):
    off = half * NT
    nch = NT // 128
    NKB = T // 512
    for k in range(KT):
        pr.dma("sp", B["kcm"][:, k * NT:(k + 1) * NT],
               B["klo_d"][k * 128:(k + 1) * 128, off:off + NT])
    pr.step()
    for pair in range(nch // 2):
        qcs = [pair * 2, pair * 2 + 1]
        for qi, qc in enumerate(qcs):
            gqc = half * 8 + qc
            pr.dma("sp", B["offq"][qi][:, :], B["off_d"][gqc:gqc + 1, :])
            pr.op("dve", "memset", B["Ob"][qi][:, :], 0.0)
            pr.op("dve", "memset", B["stats"][:, qi * 8:(qi + 1) * 8], 0.0)
        pr.step()
        pr.op("dve", "tensor_scalar", B["stats"][:, 0:1], B["stats"][:, 0:1], 0.0, -1e30, OP.mult, op1=OP.add)
        pr.op("dve", "tensor_scalar", B["stats"][:, 8:9], B["stats"][:, 8:9], 0.0, -1e30, OP.mult, op1=OP.add)
        pr.step()
        p_prev = [None, None]
        o_prev = [[None, None], [None, None]]
        pb_prev = None
        kdma = [None, None]; vdma = [None, None]
        for kb in range(NKB):
            i = 0
            kdma[i] = pr.dma("sp", B["kbuf"][i][:, :].rearrange("p (k t) -> p k t", k=KT),
                             B["kg_d"][:, kb * 512:(kb + 1) * 512].rearrange("(k p) t -> p k t", p=128),
                             nobar=True, deps=[pr.last["pe"]])
            vdma[i] = pr.dma("sp", B["vbuf"][i][:, :].rearrange("p (b c) -> p b c", b=4),
                             B["vg_d"][kb * 512:(kb + 1) * 512, :].rearrange("(b p) c -> p b c", p=128),
                             nobar=True, deps=[pr.last["pe"]])
            for qi, qc in enumerate(qcs):
                st = B["stats"][:, qi * 8:(qi + 1) * 8]
                pps = ctx.pb[0 + qi]
                mm = None
                for k in range(KT):
                    mm = pr.op("pe", "matmul", pps[:, :],
                               B["qcm"][:, k * NT + qc * 128:k * NT + (qc + 1) * 128],
                               B["kbuf"][i][:, k * 512:(k + 1) * 512],
                               start=(k == 0), stop=False, nobar=(k > 0),
                               deps=[kdma[i]] + ([p_prev[qi]] if k == 0 else []))
                mm = pr.op("pe", "matmul", pps[:, :], B["orb"][:, :],
                           B["offq"][qi][:, kb * 512:(kb + 1) * 512],
                           start=False, stop=True, deps=[mm], nobar=True)
                bmax = pr.op("dve", "reduce_max", st[:, 5:6], pps[:, :], axis=AX, deps=[mm], nobar=True)
                dd = pr.op("pool", "tensor_sub", st[:, 1:2], st[:, 5:6], st[:, 0:1], deps=[bmax], nobar=True)
                dr = pr.op("act", "activation", st[:, 2:3], st[:, 1:2], AF.Relu, deps=[dd], nobar=True)
                mnew = pr.op("pool", "tensor_add", st[:, 1:2], st[:, 0:1], st[:, 2:3], deps=[dr], nobar=True)
                aa = pr.op("pool", "tensor_sub", st[:, 2:3], st[:, 0:1], st[:, 1:2], deps=[mnew], nobar=True)
                nmg = pr.op("pool", "tensor_scalar_mul", st[:, 3:4], st[:, 1:2], -1.0, deps=[aa], nobar=True)
                mc = pr.op("pool", "tensor_copy", st[:, 0:1], st[:, 1:2], deps=[nmg], nobar=True)
                al = pr.op("act", "activation", st[:, 4:5], st[:, 2:3], AF.Exp, deps=[aa], nobar=True)
                ex = pr.op("act", "activation", B["Pb"][:, :], pps[:, :], AF.Exp,
                           bias=st[:, 3:4], deps=[nmg, al, pb_prev], nobar=True)
                p_prev[qi] = ex
                rs = pr.op("dve", "reduce_sum", st[:, 5:6], B["Pb"][:, :], axis=AX,
                           deps=[ex], nobar=True)
                bs = pr.op("pool", "tensor_copy", st[:, 7:8], st[:, 5:6], deps=[rs], nobar=True)
                lu = pr.op("dve", "scalar_tensor_tensor", st[:, 6:7], st[:, 6:7], st[:, 4:5],
                           st[:, 7:8], OP.mult, OP.add, deps=[bs], nobar=True)
                tts = []
                for tt in range(4):
                    t1 = pr.op("pe", "transpose", ctx.pbb[0][:, tt * 128:(tt + 1) * 128],
                               B["Pb"][:, tt * 128:(tt + 1) * 128], ctx.eyeb[:, :],
                               deps=[ex] + ([tts[-1]] if tts else []), nobar=True)
                    t2 = pr.op("dve", "tensor_copy", B["PbT"][:, tt * 128:(tt + 1) * 128],
                               ctx.pbb[0][:, tt * 128:(tt + 1) * 128], deps=[t1], nobar=True)
                    tts.append(t2)
                for dvh in range(2):
                    ops = ctx.pb[2 + dvh]
                    mv = None
                    for tt in range(4):
                        mv = pr.op("pe", "matmul", ops[:, :], B["PbT"][:, tt * 128:(tt + 1) * 128],
                                   B["vbuf"][i][:, tt * D + dvh * 512:tt * D + dvh * 512 + 512],
                                   start=(tt == 0), stop=(tt == 3),
                                   deps=[tts[tt], vdma[i]] + ([o_prev[0][dvh], o_prev[1][dvh]] if tt == 0 else []),
                                   nobar=True)
                    o_prev[qi][dvh] = pr.op("dve", "scalar_tensor_tensor",
                                            B["Ob"][qi][:, dvh * 512:dvh * 512 + 512],
                                            B["Ob"][qi][:, dvh * 512:dvh * 512 + 512],
                                            st[:, 4:5], ops[:, :], OP.mult, OP.add,
                                            deps=[mv, al, lu], nobar=True)
                pb_prev = mv

        pr.step()
        for qi, qc in enumerate(qcs):
            st = B["stats"][:, qi * 8:(qi + 1) * 8]
            pr.dma("sp", B["vtm"][:, :], B["vlo_d"][off + qc * 128:off + (qc + 1) * 128, :])
            pr.step()
            pps = ctx.pb[0 + qi]
            mm = None
            for k in range(KT):
                mm = pr.op("pe", "matmul", pps[:, 0:128],
                           B["qcm"][:, k * NT + qc * 128:k * NT + (qc + 1) * 128],
                           B["kcm"][:, k * NT + qc * 128:k * NT + (qc + 1) * 128],
                           start=(k == 0), stop=(k == KT - 1), nobar=(k > 0))
            pr.op("dve", "tensor_add", pps[:, 0:128], pps[:, 0:128], B["tneg"][:, :], deps=[mm])
            pr.step()
            pr.op("dve", "reduce_max", st[:, 5:6], pps[:, 0:128], axis=AX)
            pr.step()
            pr.op("pool", "tensor_sub", st[:, 1:2], st[:, 5:6], st[:, 0:1])
            pr.step()
            pr.op("act", "activation", st[:, 2:3], st[:, 1:2], AF.Relu)
            pr.step()
            pr.op("pool", "tensor_add", st[:, 1:2], st[:, 0:1], st[:, 2:3])
            pr.step()
            pr.op("pool", "tensor_sub", st[:, 2:3], st[:, 0:1], st[:, 1:2])
            pr.op("pool", "tensor_scalar_mul", st[:, 3:4], st[:, 1:2], -1.0)
            pr.step()
            al = pr.op("act", "activation", st[:, 4:5], st[:, 2:3], AF.Exp)
            ex = pr.op("act", "activation", B["Pb"][:, 0:128], pps[:, 0:128], AF.Exp,
                       bias=st[:, 3:4])
            pr.step()
            rs = pr.op("dve", "reduce_sum", st[:, 5:6], B["Pb"][:, 0:128], axis=AX)
            pr.step()
            pr.op("pool", "tensor_copy", st[:, 7:8], st[:, 5:6])
            pr.step()
            pr.op("dve", "scalar_tensor_tensor", st[:, 6:7], st[:, 6:7], st[:, 4:5],
                  st[:, 7:8], OP.mult, OP.add)
            t1 = pr.op("pe", "transpose", ctx.pbb[0][:, 0:128], B["Pb"][:, 0:128], ctx.eyeb[:, :],
                       deps=[ex])
            pr.step()
            pr.op("dve", "tensor_copy", B["PbT"][:, 0:128], ctx.pbb[0][:, 0:128])
            pr.step()
            for dvh in range(2):
                ops = ctx.pb[2 + dvh]
                mv = pr.op("pe", "matmul", ops[:, :], B["PbT"][:, 0:128],
                           B["vtm"][:, dvh * 512:dvh * 512 + 512],
                           start=True, stop=True)
                pr.op("dve", "scalar_tensor_tensor", B["Ob"][qi][:, dvh * 512:dvh * 512 + 512],
                      B["Ob"][qi][:, dvh * 512:dvh * 512 + 512], st[:, 4:5], ops[:, :],
                      OP.mult, OP.add, deps=[mv])
            pr.step()
            rc = pr.op("dve", "reciprocal", B["linv"][:, qi:qi + 1], st[:, 6:7])
            pr.step()
            lc = pr.op("pool", "tensor_copy", B["linv"][:, 2 + qi:3 + qi], B["linv"][:, qi:qi + 1])
            pr.step()
            ms = pr.op("dve", "tensor_scalar_mul", B["Ob"][qi][:, :], B["Ob"][qi][:, :],
                       B["linv"][:, 2 + qi:3 + qi], deps=[lc])
            pr.step()
            for k in range(KT):
                tps = ctx.pb[4]
                t1 = pr.op("pe", "transpose", tps[:, 0:128], B["Ob"][qi][:, k * 128:(k + 1) * 128],
                           ctx.eye[:, :], deps=[ms])
                sl = slice(k * NT + qc * 128, k * NT + (qc + 1) * 128)
                t2 = pr.op("dve", "tensor_add", B["x"][:, sl], B["x"][:, sl], tps[:, 0:128], deps=[t1])
                pr.step()
                pr.dma("sp", B["xo_d"][k * 128:(k + 1) * 128, off + qc * 128:off + (qc + 1) * 128],
                       B["x"][:, sl], deps=[t2], nobar=True)
            pr.step()

def prep_l3a(inp, core):
    def wh(W):
        return np.ascontiguousarray(W.reshape(KT, 128, KT, 128).transpose(0, 2, 3, 1))
    offv = np.zeros((16, T), np.float32)
    for qc in range(16):
        gq = core * TC + qc * 128
        offv[qc, gq:] = -1e30
    import ml_dtypes
    trilneg = ((1.0 - np.tril(np.ones((128, 128), np.float32))) * -1e30).astype(np.float32)
    return {"wkh3": wh(inp["L3_attn_Wk"]), "wqh3": wh(inp["L3_attn_Wq"]),
            "wvxh3": np.ascontiguousarray(inp["L3_attn_Wv"].T.reshape(KT, 128, D)),
            "bkc": np.ascontiguousarray(inp["L3_attn_bk"].reshape(KT, 128).T),
            "bqc": np.ascontiguousarray(inp["L3_attn_bq"].reshape(KT, 128).T),
            "bvr": inp["L3_attn_bv"].reshape(1, D),
            "offv": offv.astype(ml_dtypes.bfloat16),
            "trilneg": trilneg}

_NC_CACHE = {}

def _get_nc(name, builder):
    if name not in _NC_CACHE:
        _NC_CACHE[name] = builder()
    return _NC_CACHE[name]

def _shards_cm(x):
    return [np.ascontiguousarray(x[c * TC:(c + 1) * TC].T).astype(np.float32)
            for c in range(NCORES)]

def _halos(xs_cm):
    out = []
    for c in range(NCORES):
        if c == 0:
            out.append({"xh": np.zeros((128, KT), np.float32),
                        "hmask": np.zeros((128, 1), np.float32)})
        else:
            hal = xs_cm[c - 1][:, TC - 1]
            out.append({"xh": np.ascontiguousarray(hal.reshape(KT, 128).T),
                        "hmask": np.ones((128, 1), np.float32)})
    return out

def _run(nc, maps):
    res = run_bass_kernel_spmd(nc, maps, core_ids=list(range(NCORES)))
    return res.results

def kernel(**inputs):
    cst = host_constants()
    x0 = np.asarray(inputs["L0_attn_x"][0], np.float32)
    xs = _shards_cm(x0)

    w_moe0 = prep_moe(inputs, 0)
    w_moe1 = prep_moe(inputs, 1)
    w_l0a = prep_l0a(inputs)
    w_l1a = prep_l1a(inputs)
    w_l2f = prep_l2f(inputs, 0)
    w_l3f = [prep_l3f(inputs, c) for c in range(NCORES)]
    w_l2a = [prep_l2a(inputs, c) for c in range(NCORES)]
    w_l3a = [prep_l3a(inputs, c) for c in range(NCORES)]

    def basic_maps(extra_per_core=None, shared=None):
        maps = []
        for c in range(NCORES):
            m = dict(cst)
            if shared:
                m.update(shared)
            if extra_per_core:
                m.update(extra_per_core[c])
            m["x"] = xs[c]
            maps.append(m)
        return maps

    def upd(res):
        for c in range(NCORES):
            xs[c] = np.ascontiguousarray(np.asarray(res[c]["xo"], np.float32))

    def run_moe(layer):
        nc = _get_nc(f"moe{layer}", lambda: build_moe(layer))
        res = _run(nc, basic_maps(shared=(w_moe0 if layer == 0 else w_moe1)))
        upd(res)

    def run_l0a():
        nc1 = _get_nc("l0a1", build_l0a_p1)
        res1 = _run(nc1, basic_maps(shared=w_l0a))
        summg = np.concatenate([np.asarray(res1[c]["summ"]) for c in range(NCORES)], 1)
        nc2 = _get_nc("l0a2", build_l0a_p2)
        per = []
        for c in range(NCORES):
            mc = np.zeros((128, 8), np.float32); mc[:, :c] = 1.0
            per.append({"summg": summg, "mcol": mc})
        res = _run(nc2, basic_maps(per, shared=w_l0a))
        upd(res)

    def run_l1a():
        hm = _halos(xs)
        nc1 = _get_nc("l1a1", build_l1a_p1)
        maps = basic_maps(hm, shared=w_l1a)
        res1 = _run(nc1, maps)
        summs = [np.asarray(res1[c]["summ"]) for c in range(NCORES)]
        summg = np.zeros((128, KT * 3 * 8), np.float32)
        for k in range(KT):
            for q in range(3):
                for j in range(NCORES):
                    summg[:, (k * 3 + q) * 8 + j] = summs[j][:, k * 3 + q]
        nc2 = _get_nc("l1a2", build_l1a_p2)
        per = []
        for c in range(NCORES):
            ms = np.zeros((1, 8), np.float32)
            for j in range(NCORES):
                ms[0, j] = (c - 1 - j) * TC if j < c else 1e9
            d = dict(hm[c]); d.update({"summg": summg, "msteps": ms})
            per.append(d)
        res = _run(nc2, basic_maps(per, shared=w_l1a))
        upd(res)

    def run_l2f():
        hm = _halos(xs)
        nc = _get_nc("l2f", build_l2f)
        res = _run(nc, basic_maps(hm, shared=w_l2f))
        upd(res)

    def run_l2a():
        nc1 = _get_nc("l2a1", lambda: build_l2a(False))
        maps = []
        for c in range(NCORES):
            m = dict(cst); m.update(w_l2a[c]); m["x"] = xs[c]
            maps.append(m)
        res1 = _run(nc1, maps)
        summg = np.concatenate([np.asarray(res1[c]["summ"]) for c in range(NCORES)], 1)
        cbtgT = np.ascontiguousarray(
            np.stack([np.asarray(res1[c]["cbt"])[:, 0] for c in range(NCORES)], 0)).astype(np.float32)
        nc2 = _get_nc("l2a2", lambda: build_l2a(True))
        maps = []
        for c in range(NCORES):
            m = dict(cst); m.update(w_l2a[c]); m["x"] = xs[c]
            m["summg"] = summg; m["cbtgT"] = cbtgT
            maps.append(m)
        res = _run(nc2, maps)
        upd(res)

    def run_l3f():
        nc = _get_nc("l3f", build_l3f)
        res = _run(nc, basic_maps(w_l3f))
        upd(res)

    def run_l3a():
        nc1 = _get_nc("l3a1", build_l3a_p1)
        res1 = _run(nc1, basic_maps(w_l3a))
        kouts = [np.asarray(res1[c]["kout"]) for c in range(NCORES)]
        vouts = [np.asarray(res1[c]["vout"]) for c in range(NCORES)]
        kg = np.ascontiguousarray(np.concatenate(kouts, axis=1))
        vg = np.ascontiguousarray(np.concatenate(vouts, axis=0))
        nc2 = _get_nc("l3a2", build_l3a_p2)
        per = []
        for c in range(NCORES):
            per.append({"kg": kg, "vg": vg,
                        "klo": np.ascontiguousarray(kouts[c]),
                        "vlo": np.ascontiguousarray(vouts[c])})
            per[c].update(w_l3a[c])
        res = _run(nc2, basic_maps(per))
        upd(res)

    run_moe(0); run_l0a(); run_moe(0)
    run_moe(1); run_l1a(); run_moe(1)
    run_l2f(); run_l2a(); run_l2f()
    run_l3f(); run_l3a(); run_l3f()

    out = np.concatenate([xs[c].T for c in range(NCORES)], 0)[None, :, :]
    return out.astype(np.float32)
